# version 13
# baseline (speedup 1.0000x reference)
import numpy as np
import concourse.bass as bass
import concourse.mybir as mybir
from concourse.bass_utils import run_bass_kernel_spmd
from concourse.alu_op_type import AluOpType

F32 = mybir.dt.float32
BF16 = mybir.dt.bfloat16
AF = mybir.ActivationFunctionType
AX = mybir.AxisListType

B, N, E = 16, 2048, 1024
H, D, HALF = 16, 64, 32
NCORES = 8
SEQS = 2
CH = 8
T2 = 2
NSUB = CH * T2
PAIRS = 8
EPS = 1e-5
NCHUNKS = SEQS * CH


class Plan:
    OWN_SEM = {"tensor": "TS", "vector": "VS", "scalar": "SS"}

    def __init__(self):
        self.steps = []
        self.counts = {}
        self.milestones = {}

    def step(self, engine, fn=None, waits=(), inc=None, inc_n=1, name=None):
        self.steps.append((engine, fn, tuple(waits), inc, inc_n, name))

    def resolve(self):
        self.counts = {}
        self.milestones = {}
        for engine, fn, waits, inc, inc_n, name in self.steps:
            if inc is not None:
                self.counts[inc] = self.counts.get(inc, 0) + inc_n
                if name is not None:
                    self.milestones[name] = (inc, self.counts[inc])
        for engine, fn, waits, inc, inc_n, name in self.steps:
            for wname in waits:
                if wname not in self.milestones:
                    raise KeyError(f"unresolved milestone {wname} (engine {engine})")

    def emit(self, engine_name, eng, sems):
        watermark = {}
        own_sem = self.OWN_SEM.get(engine_name)
        own_count = 0
        for engine, fn, waits, inc, inc_n, name in self.steps:
            if engine != engine_name:
                continue
            best = {}
            for wname in waits:
                sem, val = self.milestones[wname]
                best[sem] = max(best.get(sem, 0), val)
            if own_sem is not None and own_count > 0:
                if engine_name != "tensor" or best:
                    best[own_sem] = max(best.get(own_sem, 0), own_count)
            for sem in sorted(best):
                if best[sem] > watermark.get(sem, -1):
                    eng.wait_ge(sems[sem], best[sem])
                    watermark[sem] = best[sem]
            if fn is None:
                inst = eng.nop()
            else:
                inst = fn(eng)
            if inc is not None:
                if inst is None:
                    eng.nop().then_inc(sems[inc], inc_n)
                else:
                    inst.then_inc(sems[inc], inc_n)
            if inc == own_sem:
                own_count += inc_n


def build_nc():
    nc = bass.Bass("TRN2", target_bir_lowering=False)

    x_h = nc.dram_tensor("x", [SEQS, N, E], F32, kind="ExternalInput").ap()
    out_h = nc.dram_tensor("out", [SEQS, N, E], F32, kind="ExternalOutput").ap()
    wsrc_h = [nc.dram_tensor(nm, [E, E], F32, kind="ExternalInput").ap()
              for nm in ("wq", "wk", "wv", "wo")]
    wff_h = nc.dram_tensor("wff", [E, E], F32, kind="ExternalInput").ap()
    wg1_h = nc.dram_tensor("wg1", [128, 512], F32, kind="ExternalInput").ap()
    wg2_h = nc.dram_tensor("wg2", [64, 1024], F32, kind="ExternalInput").ap()
    gam_h = nc.dram_tensor("gam", [128, E], F32, kind="ExternalInput").ap()
    bet_h = nc.dram_tensor("bet", [128, E], F32, kind="ExternalInput").ap()
    bff_h = nc.dram_tensor("bff", [128, E], F32, kind="ExternalInput").ap()
    cos_h = nc.dram_tensor("cost", [128, 512], F32, kind="ExternalInput").ap()
    sin_h = nc.dram_tensor("sint", [128, 512], F32, kind="ExternalInput").ap()
    msk_h = nc.dram_tensor("mask", [128, 128], F32, kind="ExternalInput").ap()
    id_h = nc.dram_tensor("ident", [128, 128], F32, kind="ExternalInput").ap()
    eps_h = nc.dram_tensor("epsc", [128, 1], F32, kind="ExternalInput").ap()
    ldT_h = nc.dram_tensor("ldT", [100, 4], F32, kind="ExternalInput").ap()
    luT_h = nc.dram_tensor("luT", [4, 50], F32, kind="ExternalInput").ap()
    da_h = nc.dram_tensor("da", [100, E], F32, kind="ExternalInput").ap()
    uaT_h = nc.dram_tensor("uaT", [50, E], F32, kind="ExternalInput").ap()
    whl_h = nc.dram_tensor("whl", [4, 128, 16384], BF16).ap()

    P = Plan()
    SC, TS, VS, SS = "SC", "TS", "VS", "SS"

    from contextlib import ExitStack
    stack = ExitStack()
    sb_specs = [
        ("wring", [128, 16384], BF16), ("weff16", [128, 16384], BF16),
        ("wg1b_t", [128, 1024], BF16), ("wg2b_t", [64, 2048], BF16),
        ("gams", [128, E], F32), ("bets", [128, E], F32),
        ("bffs", [128, E], F32), ("coss", [128, 512], F32),
        ("sins", [128, 512], F32),
        ("msks", [128, 128], F32), ("ids", [128, 128], F32),
        ("epss", [128, 1], F32),
        ("ldTs", [100, 4], F32), ("luTs", [4, 50], F32),
        ("xsb", [128, 4096], F32), ("outsb", [128, 4096], F32),
        ("ln1fm16", [128, 4096], BF16), ("qfm", [128, 2048], F32),
        ("kfm", [128, 2048], F32),
        ("ktm", [128, 2048], F32), ("vtm", [128, 2048], F32),
        ("gtm", [128, 2048], F32),
        ("qtm", [128, 2048], F32), ("fmsc16", [128, 4096], BF16),
        ("vsc", [128, 1024], F32),
        ("rots", [128, 1024], F32), ("ogsb", [128, 2048], F32),
        ("ssb", [128, 512], F32),
        ("zbd", [128, 1024], F32), ("g1b_t", [64, 512], BF16),
        ("stat", [128, 48], F32),
    ]
    sb = {nm: stack.enter_context(nc.sbuf_tensor(nm, shp, dt))
          for nm, shp, dt in sb_specs}
    (wring, weff16, wg1bs, wg2bs, gams, bets, bffs, coss, sins,
     msks, ids, epss,
     ldTs, luTs, xsb, outsb, ln1fm16, qfm, kfm, ktm,
     vtm, gtm, qtm, fmsc16, vsc, rots, ogsb, ssb, zbd, g1b_t, stat) = \
        (sb[nm] for nm, _, _ in sb_specs)
    wring_b = wring
    weff_b = weff16
    ln1fm_b = ln1fm16
    fmsc_b = fmsc16
    wg1b = wg1bs
    wg2b = wg2bs
    g1b = g1b_t
    wcst1 = qfm
    wcst2 = kfm
    wg1s = gtm[:, 1024:1536]
    wg2s = gtm[0:64, 0:1024]
    das = vsc[0:100, :]
    uaTs = rots[0:50, :]
    Asb = ogsb[0:4, 0:1024]
    Bsb = outsb[0:50, 0:1024]
    wtmp = outsb[:, 1024:2048]
    psA = stack.enter_context(nc.psum_tensor("psA", [128, 2048], F32))
    psB = stack.enter_context(nc.psum_tensor("psB", [128, 1024], F32))
    psO = stack.enter_context(nc.psum_tensor("psO", [128, 512], F32))
    psZ = stack.enter_context(nc.psum_tensor("psZ", [128, 512], F32))
    sem_names = ["SC", "SD", "SR0", "SR1", "SR2", "SR3", "SX0", "SX1",
                 "SO0", "SO1", "TS", "VS", "SS"]
    sem_handles = {nm: stack.enter_context(nc.semaphore(nm)) for nm in sem_names}
    with stack:
      with nc.Block() as block:
        sems = sem_handles

        def dma(engine, dst, src, sem, name=None, waits=()):
            P.step(engine, lambda e, dst=dst, src=src: e.dma_start(out=dst, in_=src),
                   waits=waits, inc=sem, inc_n=16, name=name)

        for dst, src in ((wg1s, wg1_h[:, :]), (wg2s, wg2_h[:, :]),
                         (gams[:], gam_h[:, :]), (bets[:], bet_h[:, :]),
                         (bffs[:], bff_h[:, :]), (coss[:], cos_h[:, :]),
                         (sins[:], sin_h[:, :]), (msks[:], msk_h[:, :]),
                         (ids[:], id_h[:, :]), (epss[:], eps_h[:, :]),
                         (ldTs[:], ldT_h[:, :]), (luTs[:], luT_h[:, :]),
                         (das, da_h[:, :])):
            dma("sync", dst, src, SC)
        dma("sync", uaTs, uaT_h[:, :], SC, name="consts_loaded")

        for wi in range(4):
            for hh in range(2):
                tagc = f"{wi}_{hh}"
                prev = (f"wc_cast{wi}_{hh-1}" if hh else
                        (f"wc_cast{wi-1}_1" if wi > 0 else "consts_loaded"))
                r0 = hh * 512
                dma("sync", wcst1.rearrange("p (kc o) -> p kc o", o=1024),
                    wsrc_h[wi][r0:r0 + 256, :].rearrange("(kc p) o -> p kc o", p=128),
                    SC, waits=(prev,))
                dma("sync", wcst2.rearrange("p (kc o) -> p kc o", o=1024),
                    wsrc_h[wi][r0 + 256:r0 + 512, :].rearrange(
                        "(kc p) o -> p kc o", p=128),
                    SC, name=f"wc_in{tagc}")
                hw = [f"wc_in{tagc}"]
                if wi > 0 and hh == 0:
                    hw += [f"wc_hi_out{wi-1}", f"wc_lo_out{wi-1}"]
                ho = hh * 4096
                P.step("vector",
                       lambda e, ho=ho: e.tensor_copy(wring[:, ho:ho + 2048],
                                                      wcst1[:, :]),
                       waits=tuple(hw), inc=VS)
                P.step("vector",
                       lambda e, ho=ho: e.tensor_copy(wring[:, ho + 2048:ho + 4096],
                                                      wcst2[:, :]),
                       inc=VS)
                P.step("vector",
                       lambda e, ho=ho: e.scalar_tensor_tensor(
                           wring[:, 8192 + ho:8192 + ho + 2048],
                           wring[:, ho:ho + 2048], -1.0, wcst1[:, :],
                           AluOpType.mult, AluOpType.add),
                       inc=VS)
                P.step("vector",
                       lambda e, ho=ho: e.scalar_tensor_tensor(
                           wring[:, 8192 + ho + 2048:8192 + ho + 4096],
                           wring[:, ho + 2048:ho + 4096], -1.0, wcst2[:, :],
                           AluOpType.mult, AluOpType.add),
                       inc=VS, name=f"wc_cast{tagc}")
            dma("sync", whl_h[wi][:, 0:8192], wring[:, 0:8192], "SD",
                name=f"wc_hi_out{wi}", waits=(f"wc_cast{wi}_1",))
            dma("sync", whl_h[wi][:, 8192:16384], wring[:, 8192:16384], "SD",
                name=f"wc_lo_out{wi}", waits=(f"wc_cast{wi}_1",))
        P.step("vector", lambda e: e.tensor_copy(wg1b[:, 0:512], wg1s),
               waits=("consts_loaded",), inc=VS)
        P.step("vector",
               lambda e: e.scalar_tensor_tensor(wg1b[:, 512:1024], wg1b[:, 0:512],
                                                -1.0, wg1s,
                                                AluOpType.mult, AluOpType.add),
               inc=VS)
        P.step("vector", lambda e: e.tensor_copy(wg2b[:, 0:1024], wg2s),
               inc=VS)
        P.step("vector",
               lambda e: e.scalar_tensor_tensor(wg2b[:, 1024:2048], wg2b[:, 0:1024],
                                                -1.0, wg2s,
                                                AluOpType.mult, AluOpType.add),
               inc=VS, name="wgb_done")

        for nh in range(2):
            P.step("tensor",
                   lambda e, nh=nh: e.matmul(psB[0:4, nh * 512:(nh + 1) * 512], ldTs[:],
                                             vsc[0:100, nh * 512:(nh + 1) * 512],
                                             start=True, stop=True),
                   waits=("consts_loaded",) if nh == 0 else (), inc=TS,
                   name=f"A_mm{nh}")
        P.step("vector", lambda e: e.tensor_copy(Asb, psB[0:4, 0:1024]),
               waits=("A_mm1",), inc=VS, name="A_evac")
        for nh in range(2):
            P.step("tensor",
                   lambda e, nh=nh: e.matmul(psB[0:50, nh * 512:(nh + 1) * 512], luTs[:],
                                             ogsb[0:4, nh * 512:(nh + 1) * 512],
                                             start=True, stop=True),
                   waits=("A_evac",) if nh == 0 else (), inc=TS, name=f"B_mm{nh}")
        P.step("vector", lambda e: e.tensor_copy(Bsb, psB[0:50, 0:1024]),
               waits=("B_mm1",), inc=VS, name="B_evac")

        for ic in range(8):
            src = wff_h[:, ic * 128:(ic + 1) * 128].rearrange("(oc p) i -> p oc i", p=128)
            dst = wtmp.rearrange("p (oc i) -> p oc i", i=128)
            waits = ("B_evac",) if ic == 0 else (f"weff_done{ic-1}",)
            dma("sync", dst, src, SC, name=f"wcol{ic}", waits=waits)
            for oc in range(8):
                w = (f"wcol{ic}",) if oc == 0 else ()
                P.step("tensor",
                       lambda e, oc=oc: e.transpose(psA[:, oc * 128:(oc + 1) * 128],
                                                    wtmp[:, oc * 128:(oc + 1) * 128],
                                                    ids[:]),
                       waits=w, inc=TS, name=(f"wT{ic}" if oc == 7 else None))
            for nh in range(2):
                P.step("tensor",
                       lambda e, ic=ic, nh=nh: e.matmul(
                           psA[:, 1024 + nh * 512:1024 + (nh + 1) * 512],
                           outsb[0:50, ic * 128:(ic + 1) * 128],
                           rots[0:50, nh * 512:(nh + 1) * 512], start=True, stop=True),
                       inc=TS, name=(f"cT{ic}" if nh == 1 else None))
            P.step("vector",
                   lambda e: e.tensor_copy(vsc[:, :], psA[:, 0:1024]),
                   waits=(f"wT{ic}",), inc=VS, name=f"weff_cp{ic}")
            P.step("vector",
                   lambda e: e.tensor_tensor(vsc[:, :], vsc[:, :],
                                             psA[:, 1024:2048], AluOpType.add),
                   waits=(f"cT{ic}", f"weff_cp{ic}"), inc=VS)
            P.step("vector",
                   lambda e, ic=ic: e.tensor_copy(
                       weff16[:, ic * 1024:(ic + 1) * 1024], vsc[:, :]),
                   inc=VS)
            P.step("vector",
                   lambda e, ic=ic: e.scalar_tensor_tensor(
                       weff16[:, 8192 + ic * 1024:8192 + (ic + 1) * 1024],
                       weff16[:, ic * 1024:(ic + 1) * 1024], -1.0, vsc[:, :],
                       AluOpType.mult, AluOpType.add),
                   inc=VS, name=f"weff_done{ic}")

        def emit_ln(tag, src_ap, dst_ap, sc, affine=False, sq_waits=()):
            P.step("vector",
                   lambda e, src_ap=src_ap, sc=sc: e.tensor_reduce(
                       stat[:, sc:sc + 1], src_ap, axis=AX.X, op=AluOpType.add),
                   inc=VS, name=f"{tag}_sum")
            P.step("scalar",
                   lambda e, sc=sc: e.mul(stat[:, sc + 8:sc + 9], stat[:, sc:sc + 1],
                                          -1.0 / E),
                   waits=(f"{tag}_sum",), inc=SS, name=f"{tag}_negmu")
            P.step("scalar",
                   lambda e, src_ap=src_ap, sc=sc: e.activation(
                       vsc[:], src_ap, AF.Square, bias=stat[:, sc + 8:sc + 9]),
                   waits=(f"{tag}_negmu",) + tuple(sq_waits), inc=SS, name=f"{tag}_sq")
            P.step("vector",
                   lambda e, sc=sc: e.tensor_reduce(stat[:, sc + 16:sc + 17], vsc[:],
                                                    axis=AX.X, op=AluOpType.add),
                   waits=(f"{tag}_sq",), inc=VS, name=f"{tag}_sumsq")
            P.step("scalar",
                   lambda e, sc=sc: e.activation(stat[:, sc + 24:sc + 25],
                                                 stat[:, sc + 16:sc + 17], AF.Sqrt,
                                                 scale=1.0 / E, bias=epss[:, 0:1]),
                   waits=(f"{tag}_sumsq",), inc=SS, name=f"{tag}_std")
            P.step("vector",
                   lambda e, sc=sc: e.reciprocal(stat[:, sc + 32:sc + 33],
                                                 stat[:, sc + 24:sc + 25]),
                   waits=(f"{tag}_std",), inc=VS, name=f"{tag}_rstd")
            P.step("vector",
                   lambda e, src_ap=src_ap, dst_ap=dst_ap, sc=sc: e.tensor_scalar(
                       dst_ap, src_ap, stat[:, sc + 8:sc + 9], stat[:, sc + 32:sc + 33],
                       AluOpType.add, AluOpType.mult),
                   waits=(f"{tag}_rstd",), inc=VS,
                   name=(f"{tag}_nrm0" if affine else f"{tag}_norm"))
            if affine:
                P.step("vector",
                       lambda e, dst_ap=dst_ap: e.tensor_tensor(dst_ap, dst_ap, gams[:],
                                                                AluOpType.mult),
                       waits=(f"{tag}_nrm0",), inc=VS, name=f"{tag}_g")
                P.step("vector",
                       lambda e, dst_ap=dst_ap: e.tensor_tensor(dst_ap, dst_ap, bets[:],
                                                                AluOpType.add),
                       waits=(f"{tag}_g",), inc=VS, name=f"{tag}_norm")

        def emit_transpose8(tag, src_tm, dst_cols_fn, region, extra_waits=(),
                            dual_fn=None):
            if region == "B":
                groups = [(0, 1024), (4, 1536)]
            elif region == "A01":
                groups = [(0, 0), (4, 512)]
            else:
                groups = [(0, 1024), (4, 1024)]
            for gi, (t0, poff) in enumerate(groups):
                for i in range(4):
                    ti = t0 + i
                    w = list(extra_waits) if (gi == 0 and i == 0) else []
                    if region == "B2" and gi == 1 and i == 0:
                        w.append(f"{tag}_ev0")
                    P.step("tensor",
                           lambda e, ti=ti, poff=poff, i=i, src_tm=src_tm: e.transpose(
                               psA[:, poff + i * 128:poff + (i + 1) * 128],
                               src_tm[:, ti * 128:(ti + 1) * 128], ids[:]),
                           waits=tuple(w), inc=TS,
                           name=(f"{tag}_tp{gi}" if i == 3 else None))
                if dual_fn is None:
                    dst_ap = dst_cols_fn(t0, 4)
                    P.step("scalar",
                           lambda e, dst_ap=dst_ap, poff=poff: e.copy(
                               dst_ap,
                               psA[:, poff:poff + 512].rearrange("p (i j) -> p i j", j=128)),
                           waits=(f"{tag}_tp{gi}",), inc=SS, name=f"{tag}_ev{gi}")
                else:
                    hi_ap, lo_ap = dual_fn(t0, 4)
                    P.step("scalar",
                           lambda e, hi_ap=hi_ap, poff=poff: e.copy(
                               hi_ap,
                               psA[:, poff:poff + 512].rearrange("p (i j) -> p i j", j=128)),
                           waits=(f"{tag}_tp{gi}",), inc=SS, name=f"{tag}_hi{gi}")
                    P.step("vector",
                           lambda e, hi_ap=hi_ap, lo_ap=lo_ap, poff=poff:
                           e.scalar_tensor_tensor(
                               lo_ap, hi_ap, -1.0,
                               psA[:, poff:poff + 512].rearrange("p (i j) -> p i j", j=128),
                               AluOpType.mult, AluOpType.add),
                           waits=(f"{tag}_hi{gi}",), inc=VS, name=f"{tag}_ev{gi}")

        def emit_rotary(tag, src_tile, dst_tile, s, extra_waits=()):
            sv = src_tile.rearrange("p (h d) -> p h d", d=D)
            dv = dst_tile.rearrange("p (h d) -> p h d", d=D)
            x1, x2 = sv[:, :, 0:HALF], sv[:, :, HALF:D]
            o1, o2 = dv[:, :, 0:HALF], dv[:, :, HALF:D]
            cb = coss[:, s * 32:(s + 1) * 32].unsqueeze(1).broadcast_to((128, H, 32))
            sb_ = sins[:, s * 32:(s + 1) * 32].unsqueeze(1).broadcast_to((128, H, 32))
            r1 = rots[:, 0:512].rearrange("p (h d) -> p h d", d=32)
            r2 = rots[:, 512:1024].rearrange("p (h d) -> p h d", d=32)
            r3 = ogsb[:, 0:512].rearrange("p (h d) -> p h d", d=32)
            r4 = ogsb[:, 512:1024].rearrange("p (h d) -> p h d", d=32)
            P.step("vector", lambda e, x1=x1, cb=cb, r1=r1: e.tensor_tensor(
                r1[:], x1, cb, AluOpType.mult), waits=tuple(extra_waits), inc=VS)
            P.step("vector", lambda e, x2=x2, sb_=sb_, r2=r2: e.tensor_tensor(
                r2[:], x2, sb_, AluOpType.mult), inc=VS)
            P.step("vector", lambda e, x1=x1, sb_=sb_, r3=r3: e.tensor_tensor(
                r3[:], x1, sb_, AluOpType.mult), inc=VS)
            P.step("vector", lambda e, x2=x2, cb=cb, r4=r4: e.tensor_tensor(
                r4[:], x2, cb, AluOpType.mult), inc=VS)
            P.step("vector", lambda e, o1=o1, r1=r1, r2=r2: e.tensor_tensor(
                o1, r1[:], r2[:], AluOpType.subtract), inc=VS)
            P.step("vector", lambda e, o2=o2, r3=r3, r4=r4: e.tensor_tensor(
                o2, r3[:], r4[:], AluOpType.add), inc=VS, name=f"{tag}_rot")

        def emit_proj2(tag, lhs_fn, rhs_fn, kc_waits_fn, bank_waits,
                       bank_waits_t1=(), lhs_waits_fn=None, bank_waits_nh1=()):
            for kc in range(8):
                for t in range(T2):
                    for ps in range(3):
                        for nh in range(2):
                            w = []
                            if kc == 0 and t == 0 and ps == 0 and nh == 0:
                                w += list(bank_waits)
                            if kc == 0 and t == 0 and ps == 0 and nh == 1:
                                w += list(bank_waits_nh1)
                            if kc == 0 and t == 1 and ps == 0 and nh == 0:
                                w += list(bank_waits_t1)
                            if ps == 0 and nh == 0:
                                w += kc_waits_fn(kc)
                                if lhs_waits_fn is not None and t == 0:
                                    w += lhs_waits_fn(kc)
                            lhs_ap = lhs_fn(kc, t, 0 if ps < 2 else 1)
                            rhs_ap = rhs_fn(kc, nh, 0 if ps != 1 else 1)
                            nm = None
                            if nh == 1 and ps == 2 and kc == 7:
                                nm = f"{tag}_mm_t{t}"
                            elif nh == 1 and ps == 2 and t == 1:
                                nm = f"{tag}_mm_kc{kc}"
                            P.step("tensor",
                                   lambda e, lhs_ap=lhs_ap, rhs_ap=rhs_ap, kc=kc, t=t,
                                   nh=nh, ps=ps: e.matmul(
                                       psA[:, t * 1024 + nh * 512:
                                           t * 1024 + (nh + 1) * 512],
                                       lhs_ap, rhs_ap,
                                       start=(kc == 0 and ps == 0),
                                       stop=(kc == 7 and ps == 2)),
                                   waits=tuple(w), inc=TS, name=nm)

        def emit_proj1(tag, boff, lhs_fn, rhs_fn, kc_waits_fn, bank_waits,
                       lhs_waits_fn=None, bank_waits_nh1=()):
            for kc in range(8):
                for ps in range(3):
                    for nh in range(2):
                        w = []
                        if kc == 0 and ps == 0 and nh == 0:
                            w += list(bank_waits)
                        if kc == 0 and ps == 0 and nh == 1:
                            w += list(bank_waits_nh1)
                        if ps == 0 and nh == 0:
                            w += kc_waits_fn(kc)
                            if lhs_waits_fn is not None:
                                w += lhs_waits_fn(kc)
                        lhs_ap = lhs_fn(kc, 0 if ps < 2 else 1)
                        rhs_ap = rhs_fn(kc, nh, 0 if ps != 1 else 1)
                        nm = (f"{tag}_mm_kc{kc}"
                              if (nh == 1 and ps == 2) else None)
                        P.step("tensor",
                               lambda e, lhs_ap=lhs_ap, rhs_ap=rhs_ap, kc=kc,
                               nh=nh, boff=boff, ps=ps: e.matmul(
                                   psA[:, boff + nh * 512:boff + (nh + 1) * 512],
                                   lhs_ap, rhs_ap,
                                   start=(kc == 0 and ps == 0),
                                   stop=(kc == 7 and ps == 2)),
                               waits=tuple(w), inc=TS, name=nm)

        ring_ledger = []

        def ring_load(wi, qi, free_name):
            j = len(ring_ledger)
            slot = j % 4
            waits = []
            if j >= 4:
                waits.append(ring_ledger[j - 4])
            else:
                waits.append("wc_hi_out3")
                waits.append("wc_lo_out3")
            src_hi = whl_h[wi][:, qi * 2048:(qi + 1) * 2048]
            src_lo = whl_h[wi][:, 8192 + qi * 2048:8192 + (qi + 1) * 2048]
            dst_hi = wring_b[:, slot * 4096:slot * 4096 + 2048]
            dst_lo = wring_b[:, slot * 4096 + 2048:slot * 4096 + 4096]
            dma("sync", dst_hi, src_hi, f"SR{slot}", waits=tuple(waits))
            dma("sync", dst_lo, src_lo, f"SR{slot}", name=f"wload{j}")
            ring_ledger.append(free_name)
            return j, slot

        def ln1_chain(cg2, extra_sq=()):
            for t in range(T2):
                waits = [f"xload{cg2}_1"] if t == 0 else []
                P.step("vector", None, waits=tuple(waits), inc=VS)
                sqw = list(extra_sq) if t == 0 else []
                xa = xsb[:, ((cg2 % 2) * 2 + t) * 1024:((cg2 % 2) * 2 + t + 1) * 1024]
                emit_ln(f"ln1_{cg2}_{t}", xa, qtm[:, t * 1024:(t + 1) * 1024],
                        sc=t, sq_waits=sqw)

        for t in range(T2):
            dma("gpsimd", xsb[:, t * 1024:(t + 1) * 1024],
                x_h[0, t * 128:(t + 1) * 128, :], "SX0",
                name=("xload0_1" if t == 1 else None))

        prev_pending = None
        for cg in range(NCHUNKS):
            seq, c = cg // CH, cg % CH
            xb = cg % 2
            ob = cg % 2

            def xcur_ap(t, xb=xb):
                return xsb[:, (xb * 2 + t) * 1024:(xb * 2 + t + 1) * 1024]

            qload = {}
            for wi, tag in ((0, f"qp{cg}"), (1, f"kp{cg}"), (2, f"vp{cg}")):
                for qi in range(4):
                    free = f"{tag}_mm_kc{2*qi+1}" if qi < 3 else f"{tag}_mm_t1"
                    j, slot = ring_load(wi, qi, free)
                    qload[(tag, qi)] = (j, slot)
            for tag in (f"wo{cg}a", f"wo{cg}b"):
                for qi in range(4):
                    j, slot = ring_load(3, qi, f"{tag}_mm_kc{2*qi+1}")
                    qload[(tag, qi)] = (j, slot)

            if cg + 1 < NCHUNKS:
                nseq, nc_ = (cg + 1) // CH, (cg + 1) % CH
                for t in range(T2):
                    waits = []
                    if cg >= 1 and t == 0:
                        waits.append(f"outev{cg-1}_1")
                    dma("gpsimd",
                        xsb[:, ((1 - xb) * 2 + t) * 1024:((1 - xb) * 2 + t + 1) * 1024],
                        x_h[nseq, (nc_ * 2 + t) * 128:(nc_ * 2 + t + 1) * 128, :],
                        f"SX{(cg + 1) % 2}",
                        name=(f"xload{cg+1}_1" if t == 1 else None), waits=tuple(waits))

            if c == 0:
                mw = []
                if cg > 0:
                    mw.append(f"Odone{cg-1}_1_7")
                    mw.append(f"zbdev{cg-1}_1")
                else:
                    mw.append("B_evac")
                P.step("vector", lambda e: e.memset(zbd[:], 0.0), waits=tuple(mw),
                       inc=VS)
                P.step("vector", lambda e: e.memset(psZ[0:64, :], 0.0), inc=VS)
                P.step("vector", lambda e: e.memset(psB[64:128, 512:1024], 0.0),
                       inc=VS, name=f"zbd0_{cg}")

            if cg == 0:
                ln1_chain(0, extra_sq=("A_mm1", "weff_done7"))

            for t in range(T2):
                def dual_ln1(t0, nt, t=t):
                    hi = ln1fm_b[:, 0:2048].rearrange("p (ic tj) -> p ic tj", tj=256)[
                        :, t0:t0 + nt, t * 128:(t + 1) * 128]
                    lo = ln1fm_b[:, 2048:4096].rearrange("p (ic tj) -> p ic tj", tj=256)[
                        :, t0:t0 + nt, t * 128:(t + 1) * 128]
                    return hi, lo
                ew = [f"ln1_{cg}_{t}_norm"]
                if t == 0:
                    if cg == 0:
                        ew.append("weff_done7")
                    else:
                        ew.append(f"outev{cg-1}_0")
                else:
                    ew.append(f"ln1tp{cg}_0_ev1")
                emit_transpose8(f"ln1tp{cg}_{t}", qtm[:, t * 1024:(t + 1) * 1024],
                                None, region="A01", extra_waits=ew,
                                dual_fn=dual_ln1)

            if cg > 0:
                prev_pending()

            def kc_waits_for(tag):
                def f(kc, tag=tag):
                    if kc % 2 == 0:
                        j, slot = qload[(tag, kc // 2)]
                        return [f"wload{j}"]
                    return []
                return f

            def rhs_ring(tag):
                def f(kc, nh, part, tag=tag):
                    j, slot = qload[(tag, kc // 2)]
                    off = slot * 4096 + part * 2048 + (kc % 2) * 1024 + nh * 512
                    return wring_b[:, off:off + 512]
                return f

            def lhs_ln(kc, t, part):
                off = part * 2048 + kc * 256 + t * 128
                return ln1fm_b[:, off:off + 128]

            bw = []
            bw1 = []
            if cg > 0:
                bw.append(f"outev{cg-1}_0")
                bw1.append(f"outev{cg-1}_1")

            def lhsw_ln1(kc, cg=cg):
                if kc == 0:
                    return [f"ln1tp{cg}_0_ev0", f"ln1tp{cg}_1_ev0"]
                if kc == 4:
                    return [f"ln1tp{cg}_0_ev1", f"ln1tp{cg}_1_ev1"]
                return []
            emit_proj2(f"qp{cg}", lhs_ln, rhs_ring(f"qp{cg}"), kc_waits_for(f"qp{cg}"),
                       bank_waits=bw, bank_waits_t1=bw1, lhs_waits_fn=lhsw_ln1,
                       bank_waits_nh1=[f"ln1tp{cg}_1_ev1"])
            for t in range(T2):
                P.step("scalar",
                       lambda e, t=t: e.activation(qtm[:, t * 1024:(t + 1) * 1024],
                                                   psA[:, t * 1024:(t + 1) * 1024],
                                                   AF.Silu),
                       waits=(f"qp{cg}_mm_t{t}",), inc=SS, name=f"qsilu{cg}_{t}")
            for t in range(T2):
                for kc in range(8):
                    for ps in range(3):
                        w = []
                        if t == 0 and kc == 0 and ps == 0:
                            if cg > 0:
                                w.append(f"ogev{cg-1}_1_lo")
                            w.append(f"ln1tp{cg}_1_ev1")
                            w.append("wgb_done")
                        wpart = 0 if ps != 1 else 1
                        xpart = 0 if ps < 2 else 1
                        P.step("tensor",
                               lambda e, t=t, kc=kc, ps=ps, wpart=wpart,
                               xpart=xpart: e.matmul(
                                   psO[0:64, t * 128:(t + 1) * 128],
                                   wg1b[:, wpart * 512 + kc * 64:
                                        wpart * 512 + (kc + 1) * 64],
                                   ln1fm_b[:, xpart * 2048 + kc * 256 + t * 128:
                                           xpart * 2048 + kc * 256 + (t + 1) * 128],
                                   start=(kc == 0 and ps == 0),
                                   stop=(kc == 7 and ps == 2)),
                               waits=tuple(w), inc=TS,
                               name=(f"g1mm{cg}" if (t == 1 and kc == 7 and ps == 2)
                                     else None))
            P.step("vector", lambda e: e.tensor_copy(g1b[:, 0:256], psO[0:64, 0:256]),
                   waits=(f"g1mm{cg}",), inc=VS)
            P.step("vector",
                   lambda e: e.scalar_tensor_tensor(
                       g1b[:, 256:512], g1b[:, 0:256], -1.0, psO[0:64, 0:256],
                       AluOpType.mult, AluOpType.add),
                   inc=VS, name=f"g1ev{cg}")
            for t in range(T2):
                ew = [f"qsilu{cg}_{t}", f"ln1_{cg}_1_rstd"]
                if t == 1:
                    ew.append(f"qtp{cg}_0_tp1")
                emit_rotary(f"qrot{cg}_{t}", qtm[:, t * 1024:(t + 1) * 1024], vsc,
                            2 * c + t, extra_waits=ew)
                def dst_cols_q(t0, nt, t=t):
                    return qfm.rearrange("p (pr tj) -> p pr tj", tj=256)[
                        :, t0:t0 + nt, t * 128:(t + 1) * 128]
                ew2 = [f"qrot{cg}_{t}_rot", f"qsilu{cg}_1", f"ln1tp{cg}_1_ev1"]
                if t == 1:
                    ew2.append(f"qtp{cg}_0_ev1")
                emit_transpose8(f"qtp{cg}_{t}", vsc, dst_cols_q, region="B",
                                extra_waits=ew2)

            emit_proj2(f"kp{cg}", lhs_ln, rhs_ring(f"kp{cg}"), kc_waits_for(f"kp{cg}"),
                       bank_waits=[f"qsilu{cg}_1", f"qtp{cg}_1_ev1"])
            for t in range(T2):
                w = [f"qrot{cg}_1_rot", f"kp{cg}_mm_t{t}"]
                P.step("scalar",
                       lambda e, t=t: e.activation(qtm[:, t * 1024:(t + 1) * 1024],
                                                   psA[:, t * 1024:(t + 1) * 1024],
                                                   AF.Silu),
                       waits=tuple(w), inc=SS, name=f"ksilu{cg}_{t}")
            for t in range(T2):
                ew = [f"ksilu{cg}_{t}", f"qtp{cg}_1_tp1"]
                emit_rotary(f"krot{cg}_{t}", qtm[:, t * 1024:(t + 1) * 1024],
                            ktm[:, t * 1024:(t + 1) * 1024], 2 * c + t,
                            extra_waits=ew)
            for t in range(T2):
                def dst_cols_k(t0, nt, t=t):
                    return kfm.rearrange("p (pr tj) -> p pr tj", tj=256)[
                        :, t0:t0 + nt, t * 128:(t + 1) * 128]
                ew = [f"krot{cg}_{t}_rot", f"ksilu{cg}_1", f"qtp{cg}_1_ev1"]
                if t == 1:
                    ew.append(f"ktp{cg}_0_ev1")
                emit_transpose8(f"ktp{cg}_{t}", ktm[:, t * 1024:(t + 1) * 1024],
                                dst_cols_k, region="B", extra_waits=ew)

            emit_proj2(f"vp{cg}", lhs_ln, rhs_ring(f"vp{cg}"), kc_waits_for(f"vp{cg}"),
                       bank_waits=[f"ksilu{cg}_1", f"ktp{cg}_1_ev1"])
            for t in range(T2):
                P.step("scalar",
                       lambda e, t=t: e.copy(vtm[:, t * 1024:(t + 1) * 1024],
                                             psA[:, t * 1024:(t + 1) * 1024]),
                       waits=(f"vp{cg}_mm_t{t}",), inc=SS, name=f"vev{cg}_{t}")

            for t in range(T2):
                for nh in range(2):
                    for ps in range(3):
                        w = []
                        if t == 0 and nh == 0 and ps == 0:
                            w += [f"vev{cg}_1", f"g1ev{cg}"]
                        gpart = 0 if ps < 2 else 1
                        wpart = 0 if ps != 1 else 1
                        P.step("tensor",
                               lambda e, t=t, nh=nh, ps=ps, gpart=gpart,
                               wpart=wpart: e.matmul(
                                   psA[:, t * 1024 + nh * 512:t * 1024 + (nh + 1) * 512],
                                   g1b[:, gpart * 256 + t * 128:
                                       gpart * 256 + (t + 1) * 128],
                                   wg2b[:, wpart * 1024 + nh * 512:
                                        wpart * 1024 + (nh + 1) * 512],
                                   start=(ps == 0), stop=(ps == 2)),
                               waits=tuple(w), inc=TS,
                               name=(f"g2mm{cg}" if (t == 1 and nh == 1 and ps == 2)
                                     else None))
            for t in range(T2):
                P.step("scalar",
                       lambda e, t=t: e.activation(gtm[:, t * 1024:(t + 1) * 1024],
                                                   psA[:, t * 1024:(t + 1) * 1024],
                                                   AF.Sigmoid),
                       waits=(f"g2mm{cg}",) if t == 0 else (), inc=SS,
                       name=f"gsig{cg}_{t}")


            def s_slot(p):
                if p % 2 == 0:
                    return psB[:, 0:128], psB[:, 128:256]
                return psA[:, 1024:1152], psA[:, 1152:1280]

            def ssb_slot(p):
                sp = p % 2
                return (ssb[:, sp * 256:sp * 256 + 128],
                        ssb[:, sp * 256 + 128:sp * 256 + 256])

            for t in range(T2):
                s = 2 * c + t
                base_w = [f"qtp{cg}_1_ev1", f"ktp{cg}_1_ev1", f"vev{cg}_1",
                          f"gsig{cg}_1"]
                if c == 0 and t == 0:
                    base_w.append(f"zbd0_{cg}")
                elif t == 0:
                    base_w.append(f"zbdev{cg-1}_1")
                else:
                    base_w.append(f"zbdev{cg}_0")

                def sz_block(p, t=t, base_w=base_w):
                    se_ps, so_ps = s_slot(p)
                    pcol = p * 256 + t * 128
                    wse = list(base_w) if p == 0 else []
                    if p >= 2:
                        wse.append(f"maskO{cg}_{t}_{p-2}")
                    elif t == 1:
                        wse.append(f"maskO{cg}_0_{6 + p}")
                    elif cg > 0:
                        wse.append(f"maskO{cg-1}_1_{6 + p}")
                    P.step("tensor",
                           lambda e, se_ps=se_ps, pcol=pcol: e.matmul(
                               se_ps, kfm[0:64, pcol:pcol + 128],
                               qfm[0:64, pcol:pcol + 128], start=True, stop=True),
                           waits=tuple(wse), inc=TS, name=f"Se{cg}_{t}_{p}")
                    P.step("tensor",
                           lambda e, t=t, p=p: e.matmul(
                               psZ[0:64, p * 64:(p + 1) * 64],
                               ktm[:, t * 1024 + p * 128:t * 1024 + p * 128 + 64],
                               vtm[:, t * 1024 + p * 128:t * 1024 + p * 128 + 64],
                               start=False, stop=False, skip_group_check=True),
                           inc=TS)
                    P.step("tensor",
                           lambda e, so_ps=so_ps, pcol=pcol: e.matmul(
                               so_ps, kfm[64:128, pcol:pcol + 128],
                               qfm[64:128, pcol:pcol + 128], start=True, stop=True),
                           inc=TS, name=f"So{cg}_{t}_{p}")
                    P.step("tensor",
                           lambda e, t=t, p=p: e.matmul(
                               psB[64:128, 512 + p * 64:512 + (p + 1) * 64],
                               ktm[:, t * 1024 + p * 128 + 64:t * 1024 + (p + 1) * 128],
                               vtm[:, t * 1024 + p * 128 + 64:t * 1024 + (p + 1) * 128],
                               start=False, stop=False, skip_group_check=True,
                               tile_position=(0, 64)),
                           inc=TS, name=f"Z{cg}_{t}_{p}")
                    se_ps2, so_ps2 = s_slot(p)
                    sse, sso = ssb_slot(p)
                    mw = [f"So{cg}_{t}_{p}"]
                    if p >= 2:
                        mw.append(f"Odone{cg}_{t}_{p-2}")
                    elif t == 1:
                        mw.append(f"Odone{cg}_0_{6 + p}")
                    elif cg > 0:
                        mw.append(f"Odone{cg-1}_1_{6 + p}")
                    P.step("vector",
                           lambda e, sse=sse, se_ps2=se_ps2: e.tensor_tensor(
                               sse, se_ps2, msks[:], AluOpType.mult),
                           waits=tuple(mw), inc=VS, name=f"maskE{cg}_{t}_{p}")
                    P.step("vector",
                           lambda e, sso=sso, so_ps2=so_ps2: e.tensor_tensor(
                               sso, so_ps2, msks[:], AluOpType.mult),
                           inc=VS, name=f"maskO{cg}_{t}_{p}")

                def o_block(p, t=t):
                    pcol = p * 256 + t * 128
                    sse, sso = ssb_slot(p)
                    if p < 4:
                        oreg = psO[:, p * 128:(p + 1) * 128]
                        og_guard = []
                        if p == 0:
                            og_guard.append(f"g1ev{cg}")
                            if t == 1:
                                og_guard.append(f"ogev{cg}_0_lo")
                            elif cg > 0:
                                og_guard.append(f"ogev{cg-1}_1_lo")
                    else:
                        oreg = psA[:, 1536 + (p - 4) * 128:1536 + (p - 3) * 128]
                        og_guard = []
                        if p == 4:
                            og_guard.append(f"gsig{cg}_1")
                            if t == 1:
                                og_guard.append(f"ogev{cg}_0_hi")
                            elif cg > 0:
                                og_guard.append(f"ogev{cg-1}_1_hi")
                    P.step("tensor",
                           lambda e, oreg=oreg, pcol=pcol, p=p: e.matmul(
                               oreg, qfm[:, pcol:pcol + 128],
                               zbd[:, p * 128:(p + 1) * 128], start=True, stop=False),
                           waits=tuple(og_guard), inc=TS)
                    P.step("tensor",
                           lambda e, oreg=oreg, t=t, p=p, sse=sse: e.matmul(
                               oreg[:, 0:64], sse,
                               vtm[:, t * 1024 + p * 128:t * 1024 + p * 128 + 64],
                               start=False, stop=False),
                           waits=(f"maskE{cg}_{t}_{p}",), inc=TS)
                    P.step("tensor",
                           lambda e, oreg=oreg, t=t, p=p, sso=sso: e.matmul(
                               oreg[:, 64:128], sso,
                               vtm[:, t * 1024 + p * 128 + 64:t * 1024 + (p + 1) * 128],
                               start=False, stop=True),
                           waits=(f"maskO{cg}_{t}_{p}",), inc=TS,
                           name=f"Odone{cg}_{t}_{p}")

                sz_block(0)
                sz_block(1)
                if t == 1:
                    emit_ln(f"gn{cg}_0", ogsb[:, 0:1024], vsc[:], sc=2, affine=True,
                            sq_waits=[f"qtp{cg}_1_tp1"])
                for p in range(2, PAIRS):
                    o_block(p - 2)
                    sz_block(p)
                o_block(PAIRS - 2)
                o_block(PAIRS - 1)

                P.step("vector",
                       lambda e, t=t: e.tensor_tensor(
                           ogsb[:, t * 1024:t * 1024 + 512], psO[:, 0:512],
                           gtm[:, t * 1024:t * 1024 + 512], AluOpType.mult),
                       waits=(f"Odone{cg}_{t}_3", f"gsig{cg}_{t}",
                              f"krot{cg}_1_rot"), inc=VS,
                       name=f"ogev{cg}_{t}_lo")
                P.step("vector",
                       lambda e, t=t: e.tensor_tensor(
                           ogsb[:, t * 1024 + 512:(t + 1) * 1024], psA[:, 1536:2048],
                           gtm[:, t * 1024 + 512:(t + 1) * 1024], AluOpType.mult),
                       waits=(f"Odone{cg}_{t}_7",), inc=VS, name=f"ogev{cg}_{t}_hi")
                P.step("scalar",
                       lambda e: e.copy(
                           zbd[0:64, :].rearrange("p (pr j) -> p pr j", j=128)[:, :, 0:64],
                           psZ[0:64, :].rearrange("p (pr j) -> p pr j", j=64)),
                       waits=(f"Z{cg}_{t}_7", f"Odone{cg}_{t}_7"), inc=SS)
                P.step("scalar",
                       lambda e: e.copy(
                           zbd[64:128, :].rearrange("p (pr j) -> p pr j", j=128)[:, :, 64:128],
                           psB[64:128, 512:1024].rearrange("p (pr j) -> p pr j", j=64)),
                       inc=SS, name=f"zbdev{cg}_{t}")

            def dual_og0(t0, nt):
                hi = fmsc_b[:, 0:2048].rearrange("p (ic tj) -> p ic tj", tj=256)[
                    :, t0:t0 + nt, 0:128]
                lo = fmsc_b[:, 2048:4096].rearrange("p (ic tj) -> p ic tj", tj=256)[
                    :, t0:t0 + nt, 0:128]
                return hi, lo
            emit_transpose8(f"ogtp{cg}_0", vsc, None, region="B2",
                            extra_waits=[f"gn{cg}_0_norm", f"gsig{cg}_1",
                                         f"maskO{cg}_1_7"], dual_fn=dual_og0)
            def lhsw_og0(kc, cg=cg):
                if kc == 0:
                    return [f"ogtp{cg}_0_ev0"]
                if kc == 4:
                    return [f"ogtp{cg}_0_ev1"]
                return []
            emit_proj1(f"wo{cg}a", 0,
                       lambda kc, part: fmsc_b[:, part * 2048 + kc * 256:
                                               part * 2048 + kc * 256 + 128],
                       rhs_ring(f"wo{cg}a"), kc_waits_for(f"wo{cg}a"),
                       bank_waits=[f"gsig{cg}_1"], lhs_waits_fn=lhsw_og0)
            emit_ln(f"gn{cg}_1", ogsb[:, 1024:2048], vsc[:], sc=3, affine=True,
                    sq_waits=[f"ogtp{cg}_0_tp1"])
            P.step("vector",
                   lambda e, xa=xcur_ap(0): e.tensor_tensor(
                       xa, xa, psA[:, 0:1024], AluOpType.add),
                   waits=(f"wo{cg}a_mm_kc7",), inc=VS, name=f"resev{cg}_0")
            def dual_og1(t0, nt):
                hi = fmsc_b[:, 0:2048].rearrange("p (ic tj) -> p ic tj", tj=256)[
                    :, t0:t0 + nt, 128:256]
                lo = fmsc_b[:, 2048:4096].rearrange("p (ic tj) -> p ic tj", tj=256)[
                    :, t0:t0 + nt, 128:256]
                return hi, lo
            emit_transpose8(f"ogtp{cg}_1", vsc, None, region="B2",
                            extra_waits=[f"gn{cg}_1_norm", f"ogtp{cg}_0_ev1"],
                            dual_fn=dual_og1)
            def lhsw_og1(kc, cg=cg):
                if kc == 0:
                    return [f"ogtp{cg}_1_ev0"]
                if kc == 4:
                    return [f"ogtp{cg}_1_ev1"]
                return []
            emit_proj1(f"wo{cg}b", 1024,
                       lambda kc, part: fmsc_b[:, part * 2048 + kc * 256 + 128:
                                               part * 2048 + kc * 256 + 256],
                       rhs_ring(f"wo{cg}b"), kc_waits_for(f"wo{cg}b"),
                       bank_waits=[f"ogev{cg}_1_hi", f"ogtp{cg}_1_ev1"])
            P.step("vector", None, waits=(f"resev{cg}_0",), inc=VS)
            emit_ln(f"ln2_{cg}_0", xcur_ap(0), vsc[:], sc=4,
                    sq_waits=[f"ogtp{cg}_1_tp1"])
            P.step("vector",
                   lambda e, xa=xcur_ap(1): e.tensor_tensor(
                       xa, xa, psA[:, 1024:2048], AluOpType.add),
                   waits=(f"wo{cg}b_mm_kc7",), inc=VS, name=f"resev{cg}_1")

            def rhs_weff(kc, nh, part):
                off = part * 8192 + kc * 1024 + nh * 512
                return weff_b[:, off:off + 512]

            emit_transpose8(f"ln2tp{cg}_0", vsc, None, region="B",
                            extra_waits=[f"ln2_{cg}_0_norm", f"resev{cg}_1"],
                            dual_fn=dual_og0)
            def lhsw_l20(kc, cg=cg):
                if kc == 0:
                    return [f"ln2tp{cg}_0_ev0"]
                if kc == 4:
                    return [f"ln2tp{cg}_0_ev1"]
                return []
            emit_proj1(f"ffna{cg}", 0,
                       lambda kc, part: fmsc_b[:, part * 2048 + kc * 256:
                                               part * 2048 + kc * 256 + 128],
                       rhs_weff, lambda kc: [],
                       bank_waits=[f"resev{cg}_0", "weff_done7"],
                       lhs_waits_fn=lhsw_l20)
            P.step("vector", None, waits=(f"resev{cg}_1",), inc=VS)
            emit_ln(f"ln2_{cg}_1", xcur_ap(1), vsc[:], sc=5,
                    sq_waits=[f"ln2tp{cg}_0_tp1"])
            emit_transpose8(f"ln2tp{cg}_1", vsc, None, region="B",
                            extra_waits=[f"ln2_{cg}_1_norm", f"ln2tp{cg}_0_ev1"],
                            dual_fn=dual_og1)
            def lhsw_l21(kc, cg=cg):
                if kc == 0:
                    return [f"ln2tp{cg}_1_ev0"]
                if kc == 4:
                    return [f"ln2tp{cg}_1_ev1"]
                return []
            emit_proj1(f"ffnb{cg}", 1024,
                       lambda kc, part: fmsc_b[:, part * 2048 + kc * 256 + 128:
                                               part * 2048 + kc * 256 + 256],
                       rhs_weff, lambda kc: [],
                       bank_waits=[f"resev{cg}_1"], lhs_waits_fn=lhsw_l21,
                       bank_waits_nh1=[f"ln2tp{cg}_1_ev1"])

            def emit_outev(t, cg=cg, seq=seq, c=c, ob=ob, xcur_ap=xcur_ap):
                tag = f"ffna{cg}" if t == 0 else f"ffnb{cg}"
                boff = 0 if t == 0 else 1024
                oslice = outsb[:, (ob * 2 + t) * 1024:(ob * 2 + t + 1) * 1024]
                w = [f"{tag}_mm_kc7"]
                if cg >= 2 and t == 0:
                    w.append(f"outst{cg-2}_1")
                P.step("vector",
                       lambda e, oslice=oslice, boff=boff, xa=xcur_ap(t):
                       e.scalar_tensor_tensor(
                           oslice, psA[:, boff:boff + 1024], 1.0, xa,
                           AluOpType.mult, AluOpType.add),
                       waits=tuple(w), inc=VS)
                P.step("vector",
                       lambda e, oslice=oslice: e.tensor_tensor(oslice, oslice, bffs[:],
                                                                AluOpType.add),
                       inc=VS, name=f"outev{cg}_{t}")
                dma("gpsimd", out_h[seq, (c * 2 + t) * 128:(c * 2 + t + 1) * 128, :],
                    oslice, f"SO{ob}", name=f"outst{cg}_{t}", waits=(f"outev{cg}_{t}",))
            emit_outev(0)
            if cg + 1 < NCHUNKS:
                ln1_chain(cg + 1, extra_sq=(f"ln2tp{cg}_1_tp1",))

            prev_pending = (lambda f=emit_outev: f(1))

        prev_pending()
        P.step("gpsimd", None, waits=(f"outst{NCHUNKS-2}_1", f"outst{NCHUNKS-1}_1"))

        P.resolve()
        nc._plan = P

        @block.sync
        def _(eng):
            P.emit("sync", eng, sems)

        @block.gpsimd
        def _(eng):
            P.emit("gpsimd", eng, sems)

        @block.tensor
        def _(eng):
            P.emit("tensor", eng, sems)

        @block.vector
        def _(eng):
            P.emit("vector", eng, sems)

        @block.scalar
        def _(eng):
            P.emit("scalar", eng, sems)

    return nc


_TABLES = {}


def _rope_tables():
    if "cos" in _TABLES:
        return _TABLES["cos"], _TABLES["sin"]
    import jax
    import jax.numpy as jnp
    cpu = jax.devices("cpu")[0]
    with jax.default_device(cpu):
        inv_freq = 10000.0 ** (-(jnp.arange(D // 2, dtype=jnp.float32) * 2.0) / D)
        pos = jnp.arange(N, dtype=jnp.float32)
        ang = pos[:, None] * inv_freq[None, :]
        cosf = np.asarray(jnp.cos(ang), dtype=np.float32)
        sinf = np.asarray(jnp.sin(ang), dtype=np.float32)
    _TABLES["cos"], _TABLES["sin"] = cosf, sinf
    return cosf, sinf


def _host_inputs(core, x_full, Wq, Wk, Wv, Wo, Wg1, Wg2, gamma, beta,
                 Wff, bff, lora_down, lora_up, down_aux, up_aux):
    x = np.ascontiguousarray(x_full[2 * core:2 * core + 2])
    cosf, sinf = _rope_tables()
    cos_t = np.ascontiguousarray(cosf.reshape(16, 128, 32).transpose(1, 0, 2).reshape(128, 512))
    sin_t = np.ascontiguousarray(sinf.reshape(16, 128, 32).transpose(1, 0, 2).reshape(128, 512))
    m = np.ascontiguousarray(np.triu(np.ones((128, 128), dtype=np.float32)))
    wg1r = np.ascontiguousarray(Wg1.reshape(8, 128, 64).transpose(1, 0, 2).reshape(128, 512))
    return {
        "x": x,
        "wq": np.ascontiguousarray(Wq), "wk": np.ascontiguousarray(Wk),
        "wv": np.ascontiguousarray(Wv), "wo": np.ascontiguousarray(Wo),
        "wff": np.ascontiguousarray(Wff),
        "wg1": wg1r, "wg2": np.ascontiguousarray(Wg2),
        "gam": np.ascontiguousarray(np.tile(gamma[None, :], (128, 1))),
        "bet": np.ascontiguousarray(np.tile(beta[None, :], (128, 1))),
        "bff": np.ascontiguousarray(np.tile(bff[None, :], (128, 1))),
        "cost": cos_t, "sint": sin_t, "mask": m,
        "ident": np.eye(128, dtype=np.float32),
        "epsc": np.full((128, 1), EPS, dtype=np.float32),
        "ldT": np.ascontiguousarray(lora_down.T),
        "luT": np.ascontiguousarray(lora_up.T),
        "da": np.ascontiguousarray(down_aux),
        "uaT": np.ascontiguousarray(up_aux.T),
    }


def kernel(L0_attn_x, L0_attn_Wq, L0_attn_Wk, L0_attn_Wv, L0_attn_Wo,
           L0_attn_Wg1, L0_attn_Wg2, L0_attn_gamma, L0_attn_beta,
           L0_ffn_W, L0_ffn_b, L0_ffn_lora_down, L0_ffn_lora_up,
           L0_ffn_down_aux, L0_ffn_up_aux):
    args = [np.asarray(a, dtype=np.float32) for a in
            (L0_attn_x, L0_attn_Wq, L0_attn_Wk, L0_attn_Wv, L0_attn_Wo,
             L0_attn_Wg1, L0_attn_Wg2, L0_attn_gamma, L0_attn_beta,
             L0_ffn_W, L0_ffn_b, L0_ffn_lora_down, L0_ffn_lora_up,
             L0_ffn_down_aux, L0_ffn_up_aux)]
    (x, Wq, Wk, Wv, Wo, Wg1, Wg2, gamma, beta,
     Wff, bff, ld, lu, da, ua) = args
    nc = build_nc()
    in_maps = [_host_inputs(core, x, Wq, Wk, Wv, Wo, Wg1, Wg2, gamma, beta,
                            Wff, bff, ld, lu, da, ua) for core in range(NCORES)]
    res = run_bass_kernel_spmd(nc, in_maps, core_ids=list(range(NCORES)))
    return np.concatenate([res.results[i]["out"] for i in range(NCORES)], axis=0)


# version 14
# speedup vs baseline: 1.0616x; 1.0616x over previous
import numpy as np
import concourse.bass as bass
import concourse.mybir as mybir
from concourse.bass_utils import run_bass_kernel_spmd
from concourse.alu_op_type import AluOpType

F32 = mybir.dt.float32
BF16 = mybir.dt.bfloat16
AF = mybir.ActivationFunctionType
AX = mybir.AxisListType

B, N, E = 16, 2048, 1024
H, D, HALF = 16, 64, 32
NCORES = 8
SEQS = 2
CH = 8
T2 = 2
NSUB = CH * T2
PAIRS = 8
EPS = 1e-5
NCHUNKS = SEQS * CH


class Plan:
    OWN_SEM = {"tensor": "TS", "vector": "VS", "scalar": "SS"}

    def __init__(self):
        self.steps = []
        self.counts = {}
        self.milestones = {}

    def step(self, engine, fn=None, waits=(), inc=None, inc_n=1, name=None):
        self.steps.append((engine, fn, tuple(waits), inc, inc_n, name))

    def resolve(self):
        self.counts = {}
        self.milestones = {}
        for engine, fn, waits, inc, inc_n, name in self.steps:
            if inc is not None:
                self.counts[inc] = self.counts.get(inc, 0) + inc_n
                if name is not None:
                    self.milestones[name] = (inc, self.counts[inc])
        for engine, fn, waits, inc, inc_n, name in self.steps:
            for wname in waits:
                if wname not in self.milestones:
                    raise KeyError(f"unresolved milestone {wname} (engine {engine})")

    def emit(self, engine_name, eng, sems):
        watermark = {}
        own_sem = self.OWN_SEM.get(engine_name)
        own_count = 0
        for engine, fn, waits, inc, inc_n, name in self.steps:
            if engine != engine_name:
                continue
            best = {}
            for wname in waits:
                sem, val = self.milestones[wname]
                best[sem] = max(best.get(sem, 0), val)
            if own_sem is not None and own_count > 0:
                if engine_name != "tensor" or best:
                    best[own_sem] = max(best.get(own_sem, 0), own_count)
            for sem in sorted(best):
                if best[sem] > watermark.get(sem, -1):
                    eng.wait_ge(sems[sem], best[sem])
                    watermark[sem] = best[sem]
            if fn is None:
                inst = eng.nop()
            else:
                inst = fn(eng)
            if inc is not None:
                if inst is None:
                    eng.nop().then_inc(sems[inc], inc_n)
                else:
                    inst.then_inc(sems[inc], inc_n)
            if inc == own_sem:
                own_count += inc_n


def build_nc():
    nc = bass.Bass("TRN2", target_bir_lowering=False)

    x_h = nc.dram_tensor("x", [SEQS, N, E], F32, kind="ExternalInput").ap()
    out_h = nc.dram_tensor("out", [SEQS, N, E], F32, kind="ExternalOutput").ap()
    wsrc_h = [nc.dram_tensor(nm, [E, E], F32, kind="ExternalInput").ap()
              for nm in ("wq", "wk", "wv", "wo")]
    wff_h = nc.dram_tensor("wff", [E, E], F32, kind="ExternalInput").ap()
    wg1_h = nc.dram_tensor("wg1", [128, 512], F32, kind="ExternalInput").ap()
    wg2_h = nc.dram_tensor("wg2", [64, 1024], F32, kind="ExternalInput").ap()
    gam_h = nc.dram_tensor("gam", [128, E], F32, kind="ExternalInput").ap()
    bet_h = nc.dram_tensor("bet", [128, E], F32, kind="ExternalInput").ap()
    bff_h = nc.dram_tensor("bff", [128, E], F32, kind="ExternalInput").ap()
    cos_h = nc.dram_tensor("cost", [128, 512], F32, kind="ExternalInput").ap()
    sin_h = nc.dram_tensor("sint", [128, 512], F32, kind="ExternalInput").ap()
    msk_h = nc.dram_tensor("mask", [128, 128], F32, kind="ExternalInput").ap()
    id_h = nc.dram_tensor("ident", [128, 128], F32, kind="ExternalInput").ap()
    eps_h = nc.dram_tensor("epsc", [128, 1], F32, kind="ExternalInput").ap()
    ldT_h = nc.dram_tensor("ldT", [100, 4], F32, kind="ExternalInput").ap()
    luT_h = nc.dram_tensor("luT", [4, 50], F32, kind="ExternalInput").ap()
    da_h = nc.dram_tensor("da", [100, E], F32, kind="ExternalInput").ap()
    uaT_h = nc.dram_tensor("uaT", [50, E], F32, kind="ExternalInput").ap()
    whl_h = nc.dram_tensor("whl", [4, 128, 16384], BF16).ap()

    P = Plan()
    SC, TS, VS, SS = "SC", "TS", "VS", "SS"

    from contextlib import ExitStack
    stack = ExitStack()
    sb_specs = [
        ("wring", [128, 16384], BF16), ("weff16", [128, 16384], BF16),
        ("wg1b_t", [128, 1024], BF16), ("wg2b_t", [64, 2048], BF16),
        ("gams", [128, E], F32), ("bets", [128, E], F32),
        ("bffs", [128, E], F32), ("coss", [128, 512], F32),
        ("sins", [128, 512], F32),
        ("msks", [128, 128], F32), ("ids", [128, 128], F32),
        ("epss", [128, 1], F32),
        ("ldTs", [100, 4], F32), ("luTs", [4, 50], F32),
        ("xsb", [128, 4096], F32), ("outsb", [128, 4096], F32),
        ("ln1fm16", [128, 4096], BF16), ("qfm", [128, 2048], F32),
        ("kfm", [128, 2048], F32),
        ("ktm", [128, 2048], F32), ("vtm", [128, 2048], F32),
        ("gtm", [128, 2048], F32),
        ("qtm", [128, 2048], F32), ("fmsc16", [128, 4096], BF16),
        ("vsc", [128, 1024], F32),
        ("rots", [128, 1024], F32), ("ogsb", [128, 2048], F32),
        ("ssb", [128, 512], F32),
        ("zbd", [128, 1024], F32), ("g1b_t", [64, 512], BF16),
        ("stat", [128, 48], F32),
    ]
    sb = {nm: stack.enter_context(nc.sbuf_tensor(nm, shp, dt))
          for nm, shp, dt in sb_specs}
    (wring, weff16, wg1bs, wg2bs, gams, bets, bffs, coss, sins,
     msks, ids, epss,
     ldTs, luTs, xsb, outsb, ln1fm16, qfm, kfm, ktm,
     vtm, gtm, qtm, fmsc16, vsc, rots, ogsb, ssb, zbd, g1b_t, stat) = \
        (sb[nm] for nm, _, _ in sb_specs)
    wring_b = wring
    weff_b = weff16
    ln1fm_b = ln1fm16
    fmsc_b = fmsc16
    wg1b = wg1bs
    wg2b = wg2bs
    g1b = g1b_t
    wcst1 = qfm
    wcst2 = kfm
    wg1s = gtm[:, 1024:1536]
    wg2s = gtm[0:64, 0:1024]
    das = vsc[0:100, :]
    uaTs = rots[0:50, :]
    Asb = ogsb[0:4, 0:1024]
    Bsb = outsb[0:50, 0:1024]
    wtmp = outsb[:, 1024:2048]
    psA = stack.enter_context(nc.psum_tensor("psA", [128, 2048], F32))
    psB = stack.enter_context(nc.psum_tensor("psB", [128, 1024], F32))
    psO = stack.enter_context(nc.psum_tensor("psO", [128, 512], F32))
    psZ = stack.enter_context(nc.psum_tensor("psZ", [128, 512], F32))
    sem_names = ["SC", "SD", "SR0", "SR1", "SR2", "SR3", "SX0", "SX1",
                 "SO0", "SO1", "TS", "VS", "SS"]
    sem_handles = {nm: stack.enter_context(nc.semaphore(nm)) for nm in sem_names}
    with stack:
      with nc.Block() as block:
        sems = sem_handles

        def dma(engine, dst, src, sem, name=None, waits=()):
            P.step(engine, lambda e, dst=dst, src=src: e.dma_start(out=dst, in_=src),
                   waits=waits, inc=sem, inc_n=16, name=name)

        for dst, src in ((wg1s, wg1_h[:, :]), (wg2s, wg2_h[:, :]),
                         (gams[:], gam_h[:, :]), (bets[:], bet_h[:, :]),
                         (bffs[:], bff_h[:, :]), (coss[:], cos_h[:, :]),
                         (sins[:], sin_h[:, :]), (msks[:], msk_h[:, :]),
                         (ids[:], id_h[:, :]), (epss[:], eps_h[:, :]),
                         (ldTs[:], ldT_h[:, :]), (luTs[:], luT_h[:, :]),
                         (das, da_h[:, :])):
            dma("sync", dst, src, SC)
        dma("sync", uaTs, uaT_h[:, :], SC, name="consts_loaded")

        for wi in range(4):
            for hh in range(2):
                tagc = f"{wi}_{hh}"
                prev = (f"wc_cast{wi}_{hh-1}" if hh else
                        (f"wc_cast{wi-1}_1" if wi > 0 else "consts_loaded"))
                r0 = hh * 512
                dma("sync", wcst1.rearrange("p (kc o) -> p kc o", o=1024),
                    wsrc_h[wi][r0:r0 + 256, :].rearrange("(kc p) o -> p kc o", p=128),
                    SC, waits=(prev,))
                dma("sync", wcst2.rearrange("p (kc o) -> p kc o", o=1024),
                    wsrc_h[wi][r0 + 256:r0 + 512, :].rearrange(
                        "(kc p) o -> p kc o", p=128),
                    SC, name=f"wc_in{tagc}")
                hw = [f"wc_in{tagc}"]
                if wi > 0 and hh == 0:
                    hw += [f"wc_hi_out{wi-1}", f"wc_lo_out{wi-1}"]
                ho = hh * 4096
                P.step("vector",
                       lambda e, ho=ho: e.tensor_copy(wring[:, ho:ho + 2048],
                                                      wcst1[:, :]),
                       waits=tuple(hw), inc=VS)
                P.step("vector",
                       lambda e, ho=ho: e.tensor_copy(wring[:, ho + 2048:ho + 4096],
                                                      wcst2[:, :]),
                       inc=VS)
                P.step("vector",
                       lambda e, ho=ho: e.scalar_tensor_tensor(
                           wring[:, 8192 + ho:8192 + ho + 2048],
                           wring[:, ho:ho + 2048], -1.0, wcst1[:, :],
                           AluOpType.mult, AluOpType.add),
                       inc=VS)
                P.step("vector",
                       lambda e, ho=ho: e.scalar_tensor_tensor(
                           wring[:, 8192 + ho + 2048:8192 + ho + 4096],
                           wring[:, ho + 2048:ho + 4096], -1.0, wcst2[:, :],
                           AluOpType.mult, AluOpType.add),
                       inc=VS, name=f"wc_cast{tagc}")
            dma("sync", whl_h[wi][:, 0:8192], wring[:, 0:8192], "SD",
                name=f"wc_hi_out{wi}", waits=(f"wc_cast{wi}_1",))
            dma("sync", whl_h[wi][:, 8192:16384], wring[:, 8192:16384], "SD",
                name=f"wc_lo_out{wi}", waits=(f"wc_cast{wi}_1",))
        P.step("vector", lambda e: e.tensor_copy(wg1b[:, 0:512], wg1s),
               waits=("consts_loaded",), inc=VS)
        P.step("vector",
               lambda e: e.scalar_tensor_tensor(wg1b[:, 512:1024], wg1b[:, 0:512],
                                                -1.0, wg1s,
                                                AluOpType.mult, AluOpType.add),
               inc=VS)
        P.step("vector", lambda e: e.tensor_copy(wg2b[:, 0:1024], wg2s),
               inc=VS)
        P.step("vector",
               lambda e: e.scalar_tensor_tensor(wg2b[:, 1024:2048], wg2b[:, 0:1024],
                                                -1.0, wg2s,
                                                AluOpType.mult, AluOpType.add),
               inc=VS, name="wgb_done")

        for nh in range(2):
            P.step("tensor",
                   lambda e, nh=nh: e.matmul(psB[0:4, nh * 512:(nh + 1) * 512], ldTs[:],
                                             vsc[0:100, nh * 512:(nh + 1) * 512],
                                             start=True, stop=True),
                   waits=("consts_loaded",) if nh == 0 else (), inc=TS,
                   name=f"A_mm{nh}")
        P.step("vector", lambda e: e.tensor_copy(Asb, psB[0:4, 0:1024]),
               waits=("A_mm1",), inc=VS, name="A_evac")
        for nh in range(2):
            P.step("tensor",
                   lambda e, nh=nh: e.matmul(psB[0:50, nh * 512:(nh + 1) * 512], luTs[:],
                                             ogsb[0:4, nh * 512:(nh + 1) * 512],
                                             start=True, stop=True),
                   waits=("A_evac",) if nh == 0 else (), inc=TS, name=f"B_mm{nh}")
        P.step("vector", lambda e: e.tensor_copy(Bsb, psB[0:50, 0:1024]),
               waits=("B_mm1",), inc=VS, name="B_evac")

        for ic in range(8):
            src = wff_h[:, ic * 128:(ic + 1) * 128].rearrange("(oc p) i -> p oc i", p=128)
            dst = wtmp.rearrange("p (oc i) -> p oc i", i=128)
            waits = ("B_evac",) if ic == 0 else (f"weff_done{ic-1}",)
            dma("sync", dst, src, SC, name=f"wcol{ic}", waits=waits)
            for oc in range(8):
                w = (f"wcol{ic}",) if oc == 0 else ()
                P.step("tensor",
                       lambda e, oc=oc: e.transpose(psA[:, oc * 128:(oc + 1) * 128],
                                                    wtmp[:, oc * 128:(oc + 1) * 128],
                                                    ids[:]),
                       waits=w, inc=TS, name=(f"wT{ic}" if oc == 7 else None))
            for nh in range(2):
                P.step("tensor",
                       lambda e, ic=ic, nh=nh: e.matmul(
                           psA[:, 1024 + nh * 512:1024 + (nh + 1) * 512],
                           outsb[0:50, ic * 128:(ic + 1) * 128],
                           rots[0:50, nh * 512:(nh + 1) * 512], start=True, stop=True),
                       inc=TS, name=(f"cT{ic}" if nh == 1 else None))
            P.step("vector",
                   lambda e: e.tensor_copy(vsc[:, :], psA[:, 0:1024]),
                   waits=(f"wT{ic}",), inc=VS, name=f"weff_cp{ic}")
            P.step("vector",
                   lambda e: e.tensor_tensor(vsc[:, :], vsc[:, :],
                                             psA[:, 1024:2048], AluOpType.add),
                   waits=(f"cT{ic}", f"weff_cp{ic}"), inc=VS)
            P.step("vector",
                   lambda e, ic=ic: e.tensor_copy(
                       weff16[:, ic * 1024:(ic + 1) * 1024], vsc[:, :]),
                   inc=VS)
            P.step("vector",
                   lambda e, ic=ic: e.scalar_tensor_tensor(
                       weff16[:, 8192 + ic * 1024:8192 + (ic + 1) * 1024],
                       weff16[:, ic * 1024:(ic + 1) * 1024], -1.0, vsc[:, :],
                       AluOpType.mult, AluOpType.add),
                   inc=VS, name=f"weff_done{ic}")

        def emit_ln(tag, src_ap, dst_ap, sc, affine=False, sq_waits=()):
            P.step("vector",
                   lambda e, src_ap=src_ap, sc=sc: e.tensor_reduce(
                       stat[:, sc:sc + 1], src_ap, axis=AX.X, op=AluOpType.add),
                   inc=VS, name=f"{tag}_sum")
            P.step("scalar",
                   lambda e, sc=sc: e.mul(stat[:, sc + 8:sc + 9], stat[:, sc:sc + 1],
                                          -1.0 / E),
                   waits=(f"{tag}_sum",), inc=SS, name=f"{tag}_negmu")
            P.step("scalar",
                   lambda e, src_ap=src_ap, sc=sc: e.activation(
                       vsc[:], src_ap, AF.Square, bias=stat[:, sc + 8:sc + 9]),
                   waits=(f"{tag}_negmu",) + tuple(sq_waits), inc=SS, name=f"{tag}_sq")
            P.step("vector",
                   lambda e, sc=sc: e.tensor_reduce(stat[:, sc + 16:sc + 17], vsc[:],
                                                    axis=AX.X, op=AluOpType.add),
                   waits=(f"{tag}_sq",), inc=VS, name=f"{tag}_sumsq")
            P.step("scalar",
                   lambda e, sc=sc: e.activation(stat[:, sc + 24:sc + 25],
                                                 stat[:, sc + 16:sc + 17], AF.Sqrt,
                                                 scale=1.0 / E, bias=epss[:, 0:1]),
                   waits=(f"{tag}_sumsq",), inc=SS, name=f"{tag}_std")
            P.step("vector",
                   lambda e, sc=sc: e.reciprocal(stat[:, sc + 32:sc + 33],
                                                 stat[:, sc + 24:sc + 25]),
                   waits=(f"{tag}_std",), inc=VS, name=f"{tag}_rstd")
            P.step("vector",
                   lambda e, src_ap=src_ap, dst_ap=dst_ap, sc=sc: e.tensor_scalar(
                       dst_ap, src_ap, stat[:, sc + 8:sc + 9], stat[:, sc + 32:sc + 33],
                       AluOpType.add, AluOpType.mult),
                   waits=(f"{tag}_rstd",), inc=VS,
                   name=(f"{tag}_nrm0" if affine else f"{tag}_norm"))
            if affine:
                P.step("vector",
                       lambda e, dst_ap=dst_ap: e.tensor_tensor(dst_ap, dst_ap, gams[:],
                                                                AluOpType.mult),
                       waits=(f"{tag}_nrm0",), inc=VS, name=f"{tag}_g")
                P.step("vector",
                       lambda e, dst_ap=dst_ap: e.tensor_tensor(dst_ap, dst_ap, bets[:],
                                                                AluOpType.add),
                       waits=(f"{tag}_g",), inc=VS, name=f"{tag}_norm")

        def emit_transpose8(tag, src_tm, dst_cols_fn, region, extra_waits=(),
                            dual_fn=None):
            if region == "B":
                groups = [(0, 1024), (4, 1536)]
            elif region == "A01":
                groups = [(0, 0), (4, 512)]
            else:
                groups = [(0, 1024), (4, 1024)]
            for gi, (t0, poff) in enumerate(groups):
                for i in range(4):
                    ti = t0 + i
                    w = list(extra_waits) if (gi == 0 and i == 0) else []
                    if region == "B2" and gi == 1 and i == 0:
                        w.append(f"{tag}_ev0")
                    P.step("tensor",
                           lambda e, ti=ti, poff=poff, i=i, src_tm=src_tm: e.transpose(
                               psA[:, poff + i * 128:poff + (i + 1) * 128],
                               src_tm[:, ti * 128:(ti + 1) * 128], ids[:]),
                           waits=tuple(w), inc=TS,
                           name=(f"{tag}_tp{gi}" if i == 3 else None))
                if dual_fn is None:
                    dst_ap = dst_cols_fn(t0, 4)
                    P.step("scalar",
                           lambda e, dst_ap=dst_ap, poff=poff: e.copy(
                               dst_ap,
                               psA[:, poff:poff + 512].rearrange("p (i j) -> p i j", j=128)),
                           waits=(f"{tag}_tp{gi}",), inc=SS, name=f"{tag}_ev{gi}")
                else:
                    hi_ap, lo_ap = dual_fn(t0, 4)
                    P.step("scalar",
                           lambda e, hi_ap=hi_ap, poff=poff: e.copy(
                               hi_ap,
                               psA[:, poff:poff + 512].rearrange("p (i j) -> p i j", j=128)),
                           waits=(f"{tag}_tp{gi}",), inc=SS, name=f"{tag}_hi{gi}")
                    P.step("vector",
                           lambda e, hi_ap=hi_ap, lo_ap=lo_ap, poff=poff:
                           e.scalar_tensor_tensor(
                               lo_ap, hi_ap, -1.0,
                               psA[:, poff:poff + 512].rearrange("p (i j) -> p i j", j=128),
                               AluOpType.mult, AluOpType.add),
                           waits=(f"{tag}_hi{gi}",), inc=VS, name=f"{tag}_ev{gi}")

        def emit_rotary(tag, src_tile, dst_tile, s, extra_waits=()):
            sv = src_tile.rearrange("p (h d) -> p h d", d=D)
            dv = dst_tile.rearrange("p (h d) -> p h d", d=D)
            x1, x2 = sv[:, :, 0:HALF], sv[:, :, HALF:D]
            o1, o2 = dv[:, :, 0:HALF], dv[:, :, HALF:D]
            cb = coss[:, s * 32:(s + 1) * 32].unsqueeze(1).broadcast_to((128, H, 32))
            sb_ = sins[:, s * 32:(s + 1) * 32].unsqueeze(1).broadcast_to((128, H, 32))
            r1 = rots[:, 0:512].rearrange("p (h d) -> p h d", d=32)
            r2 = rots[:, 512:1024].rearrange("p (h d) -> p h d", d=32)
            r3 = ogsb[:, 0:512].rearrange("p (h d) -> p h d", d=32)
            r4 = ogsb[:, 512:1024].rearrange("p (h d) -> p h d", d=32)
            P.step("vector", lambda e, x1=x1, cb=cb, r1=r1: e.tensor_tensor(
                r1[:], x1, cb, AluOpType.mult), waits=tuple(extra_waits), inc=VS)
            P.step("vector", lambda e, x2=x2, sb_=sb_, r2=r2: e.tensor_tensor(
                r2[:], x2, sb_, AluOpType.mult), inc=VS)
            P.step("vector", lambda e, x1=x1, sb_=sb_, r3=r3: e.tensor_tensor(
                r3[:], x1, sb_, AluOpType.mult), inc=VS)
            P.step("vector", lambda e, x2=x2, cb=cb, r4=r4: e.tensor_tensor(
                r4[:], x2, cb, AluOpType.mult), inc=VS)
            P.step("vector", lambda e, o1=o1, r1=r1, r2=r2: e.tensor_tensor(
                o1, r1[:], r2[:], AluOpType.subtract), inc=VS)
            P.step("vector", lambda e, o2=o2, r3=r3, r4=r4: e.tensor_tensor(
                o2, r3[:], r4[:], AluOpType.add), inc=VS, name=f"{tag}_rot")

        def emit_proj2(tag, lhs_fn, rhs_fn, kc_waits_fn, bank_waits,
                       bank_waits_t1=(), lhs_waits_fn=None, bank_waits_nh1=()):
            for kc in range(8):
                for t in range(T2):
                    for ps in range(3):
                        for nh in range(2):
                            w = []
                            if kc == 0 and t == 0 and ps == 0 and nh == 0:
                                w += list(bank_waits)
                            if kc == 0 and t == 0 and ps == 0 and nh == 1:
                                w += list(bank_waits_nh1)
                            if kc == 0 and t == 1 and ps == 0 and nh == 0:
                                w += list(bank_waits_t1)
                            if ps == 0 and nh == 0:
                                w += kc_waits_fn(kc)
                                if lhs_waits_fn is not None and t == 0:
                                    w += lhs_waits_fn(kc)
                            lhs_ap = lhs_fn(kc, t, 0 if ps < 2 else 1)
                            rhs_ap = rhs_fn(kc, nh, 0 if ps != 1 else 1)
                            nm = None
                            if nh == 1 and ps == 2 and kc == 7:
                                nm = f"{tag}_mm_t{t}"
                            elif nh == 1 and ps == 2 and t == 1:
                                nm = f"{tag}_mm_kc{kc}"
                            P.step("tensor",
                                   lambda e, lhs_ap=lhs_ap, rhs_ap=rhs_ap, kc=kc, t=t,
                                   nh=nh, ps=ps: e.matmul(
                                       psA[:, t * 1024 + nh * 512:
                                           t * 1024 + (nh + 1) * 512],
                                       lhs_ap, rhs_ap,
                                       start=(kc == 0 and ps == 0),
                                       stop=(kc == 7 and ps == 2)),
                                   waits=tuple(w), inc=TS, name=nm)

        def emit_proj1(tag, boff, lhs_fn, rhs_fn, kc_waits_fn, bank_waits,
                       lhs_waits_fn=None, bank_waits_nh1=()):
            for kc in range(8):
                for ps in range(3):
                    for nh in range(2):
                        w = []
                        if kc == 0 and ps == 0 and nh == 0:
                            w += list(bank_waits)
                        if kc == 0 and ps == 0 and nh == 1:
                            w += list(bank_waits_nh1)
                        if ps == 0 and nh == 0:
                            w += kc_waits_fn(kc)
                            if lhs_waits_fn is not None:
                                w += lhs_waits_fn(kc)
                        lhs_ap = lhs_fn(kc, 0 if ps < 2 else 1)
                        rhs_ap = rhs_fn(kc, nh, 0 if ps != 1 else 1)
                        nm = (f"{tag}_mm_kc{kc}"
                              if (nh == 1 and ps == 2) else None)
                        P.step("tensor",
                               lambda e, lhs_ap=lhs_ap, rhs_ap=rhs_ap, kc=kc,
                               nh=nh, boff=boff, ps=ps: e.matmul(
                                   psA[:, boff + nh * 512:boff + (nh + 1) * 512],
                                   lhs_ap, rhs_ap,
                                   start=(kc == 0 and ps == 0),
                                   stop=(kc == 7 and ps == 2)),
                               waits=tuple(w), inc=TS, name=nm)

        ring_ledger = []

        def ring_load(wi, qi, free_name):
            j = len(ring_ledger)
            slot = j % 4
            waits = []
            if j >= 4:
                waits.append(ring_ledger[j - 4])
            else:
                waits.append("wc_hi_out3")
                waits.append("wc_lo_out3")
            src_hi = whl_h[wi][:, qi * 2048:(qi + 1) * 2048]
            src_lo = whl_h[wi][:, 8192 + qi * 2048:8192 + (qi + 1) * 2048]
            dst_hi = wring_b[:, slot * 4096:slot * 4096 + 2048]
            dst_lo = wring_b[:, slot * 4096 + 2048:slot * 4096 + 4096]
            dma("sync", dst_hi, src_hi, f"SR{slot}", waits=tuple(waits))
            dma("sync", dst_lo, src_lo, f"SR{slot}", name=f"wload{j}")
            ring_ledger.append(free_name)
            return j, slot

        def ln1_chain(cg2, extra_sq=()):
            for t in range(T2):
                waits = [f"xload{cg2}_1"] if t == 0 else []
                P.step("vector", None, waits=tuple(waits), inc=VS)
                sqw = list(extra_sq) if t == 0 else []
                xa = xsb[:, ((cg2 % 2) * 2 + t) * 1024:((cg2 % 2) * 2 + t + 1) * 1024]
                emit_ln(f"ln1_{cg2}_{t}", xa, qtm[:, t * 1024:(t + 1) * 1024],
                        sc=t, sq_waits=sqw)

        for t in range(T2):
            dma("gpsimd", xsb[:, t * 1024:(t + 1) * 1024],
                x_h[0, t * 128:(t + 1) * 128, :], "SX0",
                name=("xload0_1" if t == 1 else None))

        prev_pending = None
        for cg in range(NCHUNKS):
            seq, c = cg // CH, cg % CH
            xb = cg % 2
            ob = cg % 2

            def xcur_ap(t, xb=xb):
                return xsb[:, (xb * 2 + t) * 1024:(xb * 2 + t + 1) * 1024]

            qload = {}
            for qi in range(4):
                free = (f"qp{cg}_mm_kc{2*qi+1}" if qi < 3 else f"qp{cg}_mm_t1")
                j, slot = ring_load(0, qi, free)
                qload[(f"qp{cg}", qi)] = (j, slot)
            for wi, tag in ((1, f"kp{cg}a"), (1, f"kp{cg}b"), (2, f"vp{cg}a"),
                            (2, f"vp{cg}b"), (3, f"wo{cg}a"), (3, f"wo{cg}b")):
                for qi in range(4):
                    j, slot = ring_load(wi, qi, f"{tag}_mm_kc{2*qi+1}")
                    qload[(tag, qi)] = (j, slot)

            if cg + 1 < NCHUNKS:
                nseq, nc_ = (cg + 1) // CH, (cg + 1) % CH
                for t in range(T2):
                    waits = []
                    if cg >= 1 and t == 0:
                        waits.append(f"outev{cg-1}_1")
                    dma("gpsimd",
                        xsb[:, ((1 - xb) * 2 + t) * 1024:((1 - xb) * 2 + t + 1) * 1024],
                        x_h[nseq, (nc_ * 2 + t) * 128:(nc_ * 2 + t + 1) * 128, :],
                        f"SX{(cg + 1) % 2}",
                        name=(f"xload{cg+1}_1" if t == 1 else None), waits=tuple(waits))

            if c == 0:
                mw = []
                if cg > 0:
                    mw.append(f"Odone{cg-1}_1_7")
                    mw.append(f"zbdev{cg-1}_1")
                else:
                    mw.append("B_evac")
                P.step("vector", lambda e: e.memset(zbd[:], 0.0), waits=tuple(mw),
                       inc=VS)
                P.step("vector", lambda e: e.memset(psZ[0:64, :], 0.0), inc=VS)
                P.step("vector", lambda e: e.memset(psB[64:128, 512:1024], 0.0),
                       inc=VS, name=f"zbd0_{cg}")

            if cg == 0:
                ln1_chain(0, extra_sq=("A_mm1", "weff_done7"))

            for t in range(T2):
                def dual_ln1(t0, nt, t=t):
                    hi = ln1fm_b[:, 0:2048].rearrange("p (ic tj) -> p ic tj", tj=256)[
                        :, t0:t0 + nt, t * 128:(t + 1) * 128]
                    lo = ln1fm_b[:, 2048:4096].rearrange("p (ic tj) -> p ic tj", tj=256)[
                        :, t0:t0 + nt, t * 128:(t + 1) * 128]
                    return hi, lo
                ew = [f"ln1_{cg}_{t}_norm"]
                if t == 0:
                    if cg == 0:
                        ew.append("weff_done7")
                    else:
                        ew.append(f"outev{cg-1}_0")
                else:
                    ew.append(f"ln1tp{cg}_0_ev1")
                emit_transpose8(f"ln1tp{cg}_{t}", qtm[:, t * 1024:(t + 1) * 1024],
                                None, region="A01", extra_waits=ew,
                                dual_fn=dual_ln1)

            if cg > 0:
                prev_pending()

            def kc_waits_for(tag):
                def f(kc, tag=tag):
                    if kc % 2 == 0:
                        j, slot = qload[(tag, kc // 2)]
                        return [f"wload{j}"]
                    return []
                return f

            def rhs_ring(tag):
                def f(kc, nh, part, tag=tag):
                    j, slot = qload[(tag, kc // 2)]
                    off = slot * 4096 + part * 2048 + (kc % 2) * 1024 + nh * 512
                    return wring_b[:, off:off + 512]
                return f

            def lhs_ln(kc, t, part):
                off = part * 2048 + kc * 256 + t * 128
                return ln1fm_b[:, off:off + 128]

            bw = []
            bw1 = []
            if cg > 0:
                bw.append(f"outev{cg-1}_0")
                bw1.append(f"outev{cg-1}_1")

            def lhsw_ln1(kc, cg=cg):
                if kc == 0:
                    return [f"ln1tp{cg}_0_ev0", f"ln1tp{cg}_1_ev0"]
                if kc == 4:
                    return [f"ln1tp{cg}_0_ev1", f"ln1tp{cg}_1_ev1"]
                return []
            emit_proj2(f"qp{cg}", lhs_ln, rhs_ring(f"qp{cg}"), kc_waits_for(f"qp{cg}"),
                       bank_waits=bw, bank_waits_t1=bw1, lhs_waits_fn=lhsw_ln1,
                       bank_waits_nh1=[f"ln1tp{cg}_1_ev1"])
            for t in range(T2):
                P.step("scalar",
                       lambda e, t=t: e.activation(qtm[:, t * 1024:(t + 1) * 1024],
                                                   psA[:, t * 1024:(t + 1) * 1024],
                                                   AF.Silu),
                       waits=(f"qp{cg}_mm_t{t}",), inc=SS, name=f"qsilu{cg}_{t}")
            for t in range(T2):
                for kc in range(8):
                    for ps in range(3):
                        w = []
                        if t == 0 and kc == 0 and ps == 0:
                            if cg > 0:
                                w.append(f"ogev{cg-1}_1_lo")
                            w.append(f"ln1tp{cg}_1_ev1")
                            w.append("wgb_done")
                        wpart = 0 if ps != 1 else 1
                        xpart = 0 if ps < 2 else 1
                        P.step("tensor",
                               lambda e, t=t, kc=kc, ps=ps, wpart=wpart,
                               xpart=xpart: e.matmul(
                                   psO[0:64, t * 128:(t + 1) * 128],
                                   wg1b[:, wpart * 512 + kc * 64:
                                        wpart * 512 + (kc + 1) * 64],
                                   ln1fm_b[:, xpart * 2048 + kc * 256 + t * 128:
                                           xpart * 2048 + kc * 256 + (t + 1) * 128],
                                   start=(kc == 0 and ps == 0),
                                   stop=(kc == 7 and ps == 2)),
                               waits=tuple(w), inc=TS,
                               name=(f"g1mm{cg}" if (t == 1 and kc == 7 and ps == 2)
                                     else None))
            P.step("vector", lambda e: e.tensor_copy(g1b[:, 0:256], psO[0:64, 0:256]),
                   waits=(f"g1mm{cg}",), inc=VS)
            P.step("vector",
                   lambda e: e.scalar_tensor_tensor(
                       g1b[:, 256:512], g1b[:, 0:256], -1.0, psO[0:64, 0:256],
                       AluOpType.mult, AluOpType.add),
                   inc=VS, name=f"g1ev{cg}")
            for t in range(T2):
                ew = [f"qsilu{cg}_{t}", f"ln1_{cg}_1_rstd"]
                if t == 1:
                    ew.append(f"qtp{cg}_0_tp1")
                emit_rotary(f"qrot{cg}_{t}", qtm[:, t * 1024:(t + 1) * 1024], vsc,
                            2 * c + t, extra_waits=ew)

            def lhs_ln_t(t):
                def f(kc, part, t=t):
                    off = part * 2048 + kc * 256 + t * 128
                    return ln1fm_b[:, off:off + 128]
                return f

            emit_proj1(f"kp{cg}a", 0, lhs_ln_t(0), rhs_ring(f"kp{cg}a"),
                       kc_waits_for(f"kp{cg}a"), bank_waits=[f"qsilu{cg}_0"])
            for t in range(T2):
                ew = [f"qrot{cg}_{t}_rot", f"qsilu{cg}_1"]
                if t == 1:
                    ew.append(f"qtp{cg}_0_ev1")
                def dst_cols_q(t0, nt, t=t):
                    return qfm.rearrange("p (pr tj) -> p pr tj", tj=256)[
                        :, t0:t0 + nt, t * 128:(t + 1) * 128]
                emit_transpose8(f"qtp{cg}_{t}", vsc, dst_cols_q, region="B",
                                extra_waits=ew)
            emit_proj1(f"kp{cg}b", 1024, lhs_ln_t(1), rhs_ring(f"kp{cg}b"),
                       kc_waits_for(f"kp{cg}b"),
                       bank_waits=[f"qtp{cg}_1_ev1"])
            for t in range(T2):
                tagk = f"kp{cg}a" if t == 0 else f"kp{cg}b"
                w = [f"qrot{cg}_1_rot", f"{tagk}_mm_kc7"]
                P.step("scalar",
                       lambda e, t=t: e.activation(qtm[:, t * 1024:(t + 1) * 1024],
                                                   psA[:, t * 1024:(t + 1) * 1024],
                                                   AF.Silu),
                       waits=tuple(w), inc=SS, name=f"ksilu{cg}_{t}")
            for t in range(T2):
                ew = [f"ksilu{cg}_{t}", f"qtp{cg}_1_tp1"]
                emit_rotary(f"krot{cg}_{t}", qtm[:, t * 1024:(t + 1) * 1024],
                            ktm[:, t * 1024:(t + 1) * 1024], 2 * c + t,
                            extra_waits=ew)
            emit_proj1(f"vp{cg}a", 0, lhs_ln_t(0), rhs_ring(f"vp{cg}a"),
                       kc_waits_for(f"vp{cg}a"), bank_waits=[f"ksilu{cg}_0"])
            for t in range(T2):
                def dst_cols_k(t0, nt, t=t):
                    return kfm.rearrange("p (pr tj) -> p pr tj", tj=256)[
                        :, t0:t0 + nt, t * 128:(t + 1) * 128]
                ew = [f"krot{cg}_{t}_rot", f"ksilu{cg}_1", f"qtp{cg}_1_ev1"]
                if t == 1:
                    ew.append(f"ktp{cg}_0_ev1")
                emit_transpose8(f"ktp{cg}_{t}", ktm[:, t * 1024:(t + 1) * 1024],
                                dst_cols_k, region="B", extra_waits=ew)
            emit_proj1(f"vp{cg}b", 1024, lhs_ln_t(1), rhs_ring(f"vp{cg}b"),
                       kc_waits_for(f"vp{cg}b"),
                       bank_waits=[f"ktp{cg}_1_ev1"])
            for t in range(T2):
                tagv = f"vp{cg}a" if t == 0 else f"vp{cg}b"
                P.step("scalar",
                       lambda e, t=t: e.copy(vtm[:, t * 1024:(t + 1) * 1024],
                                             psA[:, t * 1024:(t + 1) * 1024]),
                       waits=(f"{tagv}_mm_kc7",), inc=SS, name=f"vev{cg}_{t}")

            for t in range(T2):
                for nh in range(2):
                    for ps in range(3):
                        w = []
                        if t == 0 and nh == 0 and ps == 0:
                            w += [f"vev{cg}_1", f"g1ev{cg}"]
                        gpart = 0 if ps < 2 else 1
                        wpart = 0 if ps != 1 else 1
                        P.step("tensor",
                               lambda e, t=t, nh=nh, ps=ps, gpart=gpart,
                               wpart=wpart: e.matmul(
                                   psA[:, t * 1024 + nh * 512:t * 1024 + (nh + 1) * 512],
                                   g1b[:, gpart * 256 + t * 128:
                                       gpart * 256 + (t + 1) * 128],
                                   wg2b[:, wpart * 1024 + nh * 512:
                                        wpart * 1024 + (nh + 1) * 512],
                                   start=(ps == 0), stop=(ps == 2)),
                               waits=tuple(w), inc=TS,
                               name=(f"g2mm{cg}" if (t == 1 and nh == 1 and ps == 2)
                                     else None))
            for t in range(T2):
                P.step("scalar",
                       lambda e, t=t: e.activation(gtm[:, t * 1024:(t + 1) * 1024],
                                                   psA[:, t * 1024:(t + 1) * 1024],
                                                   AF.Sigmoid),
                       waits=(f"g2mm{cg}",) if t == 0 else (), inc=SS,
                       name=f"gsig{cg}_{t}")

            def s_slot(p):
                if p % 2 == 0:
                    return psB[:, 0:128], psB[:, 128:256]
                return psA[:, 1024:1152], psA[:, 1152:1280]

            def ssb_slot(p):
                sp = p % 2
                return (ssb[:, sp * 256:sp * 256 + 128],
                        ssb[:, sp * 256 + 128:sp * 256 + 256])

            for t in range(T2):
                s = 2 * c + t
                base_w = [f"qtp{cg}_1_ev1", f"ktp{cg}_1_ev1", f"vev{cg}_1",
                          f"gsig{cg}_1"]
                if c == 0 and t == 0:
                    base_w.append(f"zbd0_{cg}")
                elif t == 0:
                    base_w.append(f"zbdev{cg-1}_1")
                else:
                    base_w.append(f"zbdev{cg}_0")

                def sz_block(p, t=t, base_w=base_w):
                    se_ps, so_ps = s_slot(p)
                    pcol = p * 256 + t * 128
                    wse = list(base_w) if p == 0 else []
                    if p >= 2:
                        wse.append(f"maskO{cg}_{t}_{p-2}")
                    elif t == 1:
                        wse.append(f"maskO{cg}_0_{6 + p}")
                    elif cg > 0:
                        wse.append(f"maskO{cg-1}_1_{6 + p}")
                    P.step("tensor",
                           lambda e, se_ps=se_ps, pcol=pcol: e.matmul(
                               se_ps, kfm[0:64, pcol:pcol + 128],
                               qfm[0:64, pcol:pcol + 128], start=True, stop=True),
                           waits=tuple(wse), inc=TS, name=f"Se{cg}_{t}_{p}")
                    P.step("tensor",
                           lambda e, t=t, p=p: e.matmul(
                               psZ[0:64, p * 64:(p + 1) * 64],
                               ktm[:, t * 1024 + p * 128:t * 1024 + p * 128 + 64],
                               vtm[:, t * 1024 + p * 128:t * 1024 + p * 128 + 64],
                               start=False, stop=False, skip_group_check=True),
                           inc=TS)
                    P.step("tensor",
                           lambda e, so_ps=so_ps, pcol=pcol: e.matmul(
                               so_ps, kfm[64:128, pcol:pcol + 128],
                               qfm[64:128, pcol:pcol + 128], start=True, stop=True),
                           inc=TS, name=f"So{cg}_{t}_{p}")
                    P.step("tensor",
                           lambda e, t=t, p=p: e.matmul(
                               psB[64:128, 512 + p * 64:512 + (p + 1) * 64],
                               ktm[:, t * 1024 + p * 128 + 64:t * 1024 + (p + 1) * 128],
                               vtm[:, t * 1024 + p * 128 + 64:t * 1024 + (p + 1) * 128],
                               start=False, stop=False, skip_group_check=True,
                               tile_position=(0, 64)),
                           inc=TS, name=f"Z{cg}_{t}_{p}")
                    se_ps2, so_ps2 = s_slot(p)
                    sse, sso = ssb_slot(p)
                    mw = [f"So{cg}_{t}_{p}"]
                    if p >= 2:
                        mw.append(f"Odone{cg}_{t}_{p-2}")
                    elif t == 1:
                        mw.append(f"Odone{cg}_0_{6 + p}")
                    elif cg > 0:
                        mw.append(f"Odone{cg-1}_1_{6 + p}")
                    P.step("vector",
                           lambda e, sse=sse, se_ps2=se_ps2: e.tensor_tensor(
                               sse, se_ps2, msks[:], AluOpType.mult),
                           waits=tuple(mw), inc=VS, name=f"maskE{cg}_{t}_{p}")
                    P.step("vector",
                           lambda e, sso=sso, so_ps2=so_ps2: e.tensor_tensor(
                               sso, so_ps2, msks[:], AluOpType.mult),
                           inc=VS, name=f"maskO{cg}_{t}_{p}")

                def o_block(p, t=t):
                    pcol = p * 256 + t * 128
                    sse, sso = ssb_slot(p)
                    if p < 4:
                        oreg = psO[:, p * 128:(p + 1) * 128]
                        og_guard = []
                        if p == 0:
                            og_guard.append(f"g1ev{cg}")
                            if t == 1:
                                og_guard.append(f"ogev{cg}_0_lo")
                            elif cg > 0:
                                og_guard.append(f"ogev{cg-1}_1_lo")
                    else:
                        oreg = psA[:, 1536 + (p - 4) * 128:1536 + (p - 3) * 128]
                        og_guard = []
                        if p == 4:
                            og_guard.append(f"gsig{cg}_1")
                            if t == 1:
                                og_guard.append(f"ogev{cg}_0_hi")
                            elif cg > 0:
                                og_guard.append(f"ogev{cg-1}_1_hi")
                    P.step("tensor",
                           lambda e, oreg=oreg, pcol=pcol, p=p: e.matmul(
                               oreg, qfm[:, pcol:pcol + 128],
                               zbd[:, p * 128:(p + 1) * 128], start=True, stop=False),
                           waits=tuple(og_guard), inc=TS)
                    P.step("tensor",
                           lambda e, oreg=oreg, t=t, p=p, sse=sse: e.matmul(
                               oreg[:, 0:64], sse,
                               vtm[:, t * 1024 + p * 128:t * 1024 + p * 128 + 64],
                               start=False, stop=False),
                           waits=(f"maskE{cg}_{t}_{p}",), inc=TS)
                    P.step("tensor",
                           lambda e, oreg=oreg, t=t, p=p, sso=sso: e.matmul(
                               oreg[:, 64:128], sso,
                               vtm[:, t * 1024 + p * 128 + 64:t * 1024 + (p + 1) * 128],
                               start=False, stop=True),
                           waits=(f"maskO{cg}_{t}_{p}",), inc=TS,
                           name=f"Odone{cg}_{t}_{p}")

                sz_block(0)
                sz_block(1)
                if t == 1:
                    emit_ln(f"gn{cg}_0", ogsb[:, 0:1024], vsc[:], sc=2, affine=True,
                            sq_waits=[f"qtp{cg}_1_tp1"])
                for p in range(2, PAIRS):
                    o_block(p - 2)
                    sz_block(p)
                o_block(PAIRS - 2)
                o_block(PAIRS - 1)

                P.step("vector",
                       lambda e, t=t: e.tensor_tensor(
                           ogsb[:, t * 1024:t * 1024 + 512], psO[:, 0:512],
                           gtm[:, t * 1024:t * 1024 + 512], AluOpType.mult),
                       waits=(f"Odone{cg}_{t}_3", f"gsig{cg}_{t}",
                              f"krot{cg}_1_rot"), inc=VS,
                       name=f"ogev{cg}_{t}_lo")
                P.step("vector",
                       lambda e, t=t: e.tensor_tensor(
                           ogsb[:, t * 1024 + 512:(t + 1) * 1024], psA[:, 1536:2048],
                           gtm[:, t * 1024 + 512:(t + 1) * 1024], AluOpType.mult),
                       waits=(f"Odone{cg}_{t}_7",), inc=VS, name=f"ogev{cg}_{t}_hi")
                P.step("scalar",
                       lambda e: e.copy(
                           zbd[0:64, :].rearrange("p (pr j) -> p pr j", j=128)[:, :, 0:64],
                           psZ[0:64, :].rearrange("p (pr j) -> p pr j", j=64)),
                       waits=(f"Z{cg}_{t}_7", f"Odone{cg}_{t}_7"), inc=SS)
                P.step("scalar",
                       lambda e: e.copy(
                           zbd[64:128, :].rearrange("p (pr j) -> p pr j", j=128)[:, :, 64:128],
                           psB[64:128, 512:1024].rearrange("p (pr j) -> p pr j", j=64)),
                       inc=SS, name=f"zbdev{cg}_{t}")

            def dual_og0(t0, nt):
                hi = fmsc_b[:, 0:2048].rearrange("p (ic tj) -> p ic tj", tj=256)[
                    :, t0:t0 + nt, 0:128]
                lo = fmsc_b[:, 2048:4096].rearrange("p (ic tj) -> p ic tj", tj=256)[
                    :, t0:t0 + nt, 0:128]
                return hi, lo
            emit_transpose8(f"ogtp{cg}_0", vsc, None, region="B2",
                            extra_waits=[f"gn{cg}_0_norm", f"gsig{cg}_1",
                                         f"maskO{cg}_1_7"], dual_fn=dual_og0)
            def lhsw_og0(kc, cg=cg):
                if kc == 0:
                    return [f"ogtp{cg}_0_ev0"]
                if kc == 4:
                    return [f"ogtp{cg}_0_ev1"]
                return []
            emit_proj1(f"wo{cg}a", 0,
                       lambda kc, part: fmsc_b[:, part * 2048 + kc * 256:
                                               part * 2048 + kc * 256 + 128],
                       rhs_ring(f"wo{cg}a"), kc_waits_for(f"wo{cg}a"),
                       bank_waits=[f"gsig{cg}_1"], lhs_waits_fn=lhsw_og0)
            emit_ln(f"gn{cg}_1", ogsb[:, 1024:2048], vsc[:], sc=3, affine=True,
                    sq_waits=[f"ogtp{cg}_0_tp1"])
            P.step("vector",
                   lambda e, xa=xcur_ap(0): e.tensor_tensor(
                       xa, xa, psA[:, 0:1024], AluOpType.add),
                   waits=(f"wo{cg}a_mm_kc7",), inc=VS, name=f"resev{cg}_0")
            def dual_og1(t0, nt):
                hi = fmsc_b[:, 0:2048].rearrange("p (ic tj) -> p ic tj", tj=256)[
                    :, t0:t0 + nt, 128:256]
                lo = fmsc_b[:, 2048:4096].rearrange("p (ic tj) -> p ic tj", tj=256)[
                    :, t0:t0 + nt, 128:256]
                return hi, lo
            emit_transpose8(f"ogtp{cg}_1", vsc, None, region="B2",
                            extra_waits=[f"gn{cg}_1_norm", f"ogtp{cg}_0_ev1"],
                            dual_fn=dual_og1)
            def lhsw_og1(kc, cg=cg):
                if kc == 0:
                    return [f"ogtp{cg}_1_ev0"]
                if kc == 4:
                    return [f"ogtp{cg}_1_ev1"]
                return []
            emit_proj1(f"wo{cg}b", 1024,
                       lambda kc, part: fmsc_b[:, part * 2048 + kc * 256 + 128:
                                               part * 2048 + kc * 256 + 256],
                       rhs_ring(f"wo{cg}b"), kc_waits_for(f"wo{cg}b"),
                       bank_waits=[f"ogev{cg}_1_hi", f"ogtp{cg}_1_ev1"])
            P.step("vector", None, waits=(f"resev{cg}_0",), inc=VS)
            emit_ln(f"ln2_{cg}_0", xcur_ap(0), vsc[:], sc=4,
                    sq_waits=[f"ogtp{cg}_1_tp1"])
            P.step("vector",
                   lambda e, xa=xcur_ap(1): e.tensor_tensor(
                       xa, xa, psA[:, 1024:2048], AluOpType.add),
                   waits=(f"wo{cg}b_mm_kc7",), inc=VS, name=f"resev{cg}_1")

            def rhs_weff(kc, nh, part):
                off = part * 8192 + kc * 1024 + nh * 512
                return weff_b[:, off:off + 512]

            emit_transpose8(f"ln2tp{cg}_0", vsc, None, region="B",
                            extra_waits=[f"ln2_{cg}_0_norm", f"resev{cg}_1"],
                            dual_fn=dual_og0)
            def lhsw_l20(kc, cg=cg):
                if kc == 0:
                    return [f"ln2tp{cg}_0_ev0"]
                if kc == 4:
                    return [f"ln2tp{cg}_0_ev1"]
                return []
            emit_proj1(f"ffna{cg}", 0,
                       lambda kc, part: fmsc_b[:, part * 2048 + kc * 256:
                                               part * 2048 + kc * 256 + 128],
                       rhs_weff, lambda kc: [],
                       bank_waits=[f"resev{cg}_0", "weff_done7"],
                       lhs_waits_fn=lhsw_l20)
            P.step("vector", None, waits=(f"resev{cg}_1",), inc=VS)
            emit_ln(f"ln2_{cg}_1", xcur_ap(1), vsc[:], sc=5,
                    sq_waits=[f"ln2tp{cg}_0_tp1"])
            emit_transpose8(f"ln2tp{cg}_1", vsc, None, region="B",
                            extra_waits=[f"ln2_{cg}_1_norm", f"ln2tp{cg}_0_ev1"],
                            dual_fn=dual_og1)
            def lhsw_l21(kc, cg=cg):
                if kc == 0:
                    return [f"ln2tp{cg}_1_ev0"]
                if kc == 4:
                    return [f"ln2tp{cg}_1_ev1"]
                return []
            emit_proj1(f"ffnb{cg}", 1024,
                       lambda kc, part: fmsc_b[:, part * 2048 + kc * 256 + 128:
                                               part * 2048 + kc * 256 + 256],
                       rhs_weff, lambda kc: [],
                       bank_waits=[f"resev{cg}_1"], lhs_waits_fn=lhsw_l21,
                       bank_waits_nh1=[f"ln2tp{cg}_1_ev1"])

            def emit_outev(t, cg=cg, seq=seq, c=c, ob=ob, xcur_ap=xcur_ap):
                tag = f"ffna{cg}" if t == 0 else f"ffnb{cg}"
                boff = 0 if t == 0 else 1024
                oslice = outsb[:, (ob * 2 + t) * 1024:(ob * 2 + t + 1) * 1024]
                w = [f"{tag}_mm_kc7"]
                if cg >= 2 and t == 0:
                    w.append(f"outst{cg-2}_1")
                P.step("vector",
                       lambda e, oslice=oslice, boff=boff, xa=xcur_ap(t):
                       e.scalar_tensor_tensor(
                           oslice, psA[:, boff:boff + 1024], 1.0, xa,
                           AluOpType.mult, AluOpType.add),
                       waits=tuple(w), inc=VS)
                P.step("vector",
                       lambda e, oslice=oslice: e.tensor_tensor(oslice, oslice, bffs[:],
                                                                AluOpType.add),
                       inc=VS, name=f"outev{cg}_{t}")
                dma("gpsimd", out_h[seq, (c * 2 + t) * 128:(c * 2 + t + 1) * 128, :],
                    oslice, f"SO{ob}", name=f"outst{cg}_{t}", waits=(f"outev{cg}_{t}",))
            emit_outev(0)
            if cg + 1 < NCHUNKS:
                ln1_chain(cg + 1, extra_sq=(f"ln2tp{cg}_1_tp1",))

            prev_pending = (lambda f=emit_outev: f(1))

        prev_pending()
        P.step("gpsimd", None, waits=(f"outst{NCHUNKS-2}_1", f"outst{NCHUNKS-1}_1"))

        P.resolve()
        nc._plan = P

        @block.sync
        def _(eng):
            P.emit("sync", eng, sems)

        @block.gpsimd
        def _(eng):
            P.emit("gpsimd", eng, sems)

        @block.tensor
        def _(eng):
            P.emit("tensor", eng, sems)

        @block.vector
        def _(eng):
            P.emit("vector", eng, sems)

        @block.scalar
        def _(eng):
            P.emit("scalar", eng, sems)

    return nc


_TABLES = {}


def _rope_tables():
    if "cos" in _TABLES:
        return _TABLES["cos"], _TABLES["sin"]
    import jax
    import jax.numpy as jnp
    cpu = jax.devices("cpu")[0]
    with jax.default_device(cpu):
        inv_freq = 10000.0 ** (-(jnp.arange(D // 2, dtype=jnp.float32) * 2.0) / D)
        pos = jnp.arange(N, dtype=jnp.float32)
        ang = pos[:, None] * inv_freq[None, :]
        cosf = np.asarray(jnp.cos(ang), dtype=np.float32)
        sinf = np.asarray(jnp.sin(ang), dtype=np.float32)
    _TABLES["cos"], _TABLES["sin"] = cosf, sinf
    return cosf, sinf


def _host_inputs(core, x_full, Wq, Wk, Wv, Wo, Wg1, Wg2, gamma, beta,
                 Wff, bff, lora_down, lora_up, down_aux, up_aux):
    x = np.ascontiguousarray(x_full[2 * core:2 * core + 2])
    cosf, sinf = _rope_tables()
    cos_t = np.ascontiguousarray(cosf.reshape(16, 128, 32).transpose(1, 0, 2).reshape(128, 512))
    sin_t = np.ascontiguousarray(sinf.reshape(16, 128, 32).transpose(1, 0, 2).reshape(128, 512))
    m = np.ascontiguousarray(np.triu(np.ones((128, 128), dtype=np.float32)))
    wg1r = np.ascontiguousarray(Wg1.reshape(8, 128, 64).transpose(1, 0, 2).reshape(128, 512))
    return {
        "x": x,
        "wq": np.ascontiguousarray(Wq), "wk": np.ascontiguousarray(Wk),
        "wv": np.ascontiguousarray(Wv), "wo": np.ascontiguousarray(Wo),
        "wff": np.ascontiguousarray(Wff),
        "wg1": wg1r, "wg2": np.ascontiguousarray(Wg2),
        "gam": np.ascontiguousarray(np.tile(gamma[None, :], (128, 1))),
        "bet": np.ascontiguousarray(np.tile(beta[None, :], (128, 1))),
        "bff": np.ascontiguousarray(np.tile(bff[None, :], (128, 1))),
        "cost": cos_t, "sint": sin_t, "mask": m,
        "ident": np.eye(128, dtype=np.float32),
        "epsc": np.full((128, 1), EPS, dtype=np.float32),
        "ldT": np.ascontiguousarray(lora_down.T),
        "luT": np.ascontiguousarray(lora_up.T),
        "da": np.ascontiguousarray(down_aux),
        "uaT": np.ascontiguousarray(up_aux.T),
    }


def kernel(L0_attn_x, L0_attn_Wq, L0_attn_Wk, L0_attn_Wv, L0_attn_Wo,
           L0_attn_Wg1, L0_attn_Wg2, L0_attn_gamma, L0_attn_beta,
           L0_ffn_W, L0_ffn_b, L0_ffn_lora_down, L0_ffn_lora_up,
           L0_ffn_down_aux, L0_ffn_up_aux):
    args = [np.asarray(a, dtype=np.float32) for a in
            (L0_attn_x, L0_attn_Wq, L0_attn_Wk, L0_attn_Wv, L0_attn_Wo,
             L0_attn_Wg1, L0_attn_Wg2, L0_attn_gamma, L0_attn_beta,
             L0_ffn_W, L0_ffn_b, L0_ffn_lora_down, L0_ffn_lora_up,
             L0_ffn_down_aux, L0_ffn_up_aux)]
    (x, Wq, Wk, Wv, Wo, Wg1, Wg2, gamma, beta,
     Wff, bff, ld, lu, da, ua) = args
    nc = build_nc()
    in_maps = [_host_inputs(core, x, Wq, Wk, Wv, Wo, Wg1, Wg2, gamma, beta,
                            Wff, bff, ld, lu, da, ua) for core in range(NCORES)]
    res = run_bass_kernel_spmd(nc, in_maps, core_ids=list(range(NCORES)))
    return np.concatenate([res.results[i]["out"] for i in range(NCORES)], axis=0)
